# version 08 (final)
# speedup vs baseline: 1.2486x; 1.0581x over previous
import numpy as np
from contextlib import ExitStack

import concourse.bass as bass
import concourse.mybir as mybir
from concourse.bass_utils import run_bass_kernel_spmd

F32 = mybir.dt.float32
BF16 = mybir.dt.bfloat16
AF = mybir.ActivationFunctionType
ALU = mybir.AluOpType
AX = mybir.AxisListType

D = 2048
S = 2048
NSEQ = 2
NH = 16
KVR = 512
NOPE = 128
ROPE = 64
QKH = NOPE + ROPE
VD = 128
HID = 8192
GSZ = 512
NG = NSEQ * S // GSZ
NC = 16
L0_SCALE = QKH ** (-0.5)
L1_SCALE = 1.0 / np.float32(np.sqrt(128.0))
EPS_LN = 1e-5
EPS_RMS = 1e-6
NEG_INF = -1e30
PI_LO = float(np.nextafter(np.float32(np.pi), np.float32(0)))

ENGINES = ('sync', 'gpsimd', 'scalar', 'vector', 'tensor')



def host_prep(i):
    C = np.ascontiguousarray
    sh = {}
    wq_h = i['L0_attn_wq'].reshape(NH, QKH, D)
    sh['wqn_t'] = C(wq_h[:, :NOPE, :].reshape(NH * NOPE, D).T)
    sh['wqp_t'] = C(wq_h[:, NOPE:, :].reshape(NH * ROPE, D).T)
    sh['wkva_t'] = C(i['L0_attn_wkv_a'].T)
    wkvb_h = i['L0_attn_wkv_b'].reshape(NH, NOPE + VD, KVR)
    sh['w1'] = C(wkvb_h[:, :NOPE, :])
    sh['w2t'] = C(wkvb_h[:, NOPE:, :].transpose(0, 2, 1))
    sh['wo_t'] = C(i['L0_attn_wo'].T)
    sh['wg_t'] = C(i['L0_ffn_w_gate'].T)
    sh['wu_t'] = C(i['L0_ffn_w_up'].T)
    sh['wd_t'] = C(i['L0_ffn_w_down'].T)
    sh['l1wq'] = C(i['L1_attn_Wq'])
    sh['l1wk'] = C(i['L1_attn_Wk'])
    sh['l1wv'] = C(i['L1_attn_Wv'])
    sh['l1wo'] = C(i['L1_attn_Wo'])
    sh['ampw'] = C(i['L1_ffn_amp_w'])
    sh['phw'] = C(i['L1_ffn_ph_w'])
    sh['entw0'] = C(i['L1_ffn_ent_w'][0])
    sh['entw1'] = C(i['L1_ffn_ent_w'][1])
    sh['entpw0'] = C(i['L1_ffn_ent_pw'][0])
    sh['entpw1'] = C(i['L1_ffn_ent_pw'][1])
    sh['pathw'] = C(i['L1_ffn_path_w'])

    def park(v):
        return C(v.reshape(-1, 128).T)
    sh['kvw_row'] = C(i['L0_attn_kv_norm_w'].reshape(1, KVR))
    sh['bq_c'] = park(i['L1_attn_bq']); sh['bk_c'] = park(i['L1_attn_bk'])
    sh['bo_c'] = park(i['L1_attn_bo'])
    sh['bv_row'] = C(i['L1_attn_bv'].reshape(1, D))
    sh['ampb_c'] = park(i['L1_ffn_amp_b']); sh['phb_c'] = park(i['L1_ffn_ph_b'])
    sh['entb_c'] = C(np.concatenate([park(i['L1_ffn_ent_b'][l]) for l in range(2)], 1))
    sh['entpb_c'] = C(np.concatenate([park(i['L1_ffn_ent_pb'][l]) for l in range(2)], 1))
    sh['ln3g_c'] = C(np.concatenate([park(i['L1_ffn_ln3_g'][l]) for l in range(2)], 1))
    sh['ln3b_c'] = C(np.concatenate([park(i['L1_ffn_ln3_b'][l]) for l in range(2)], 1))
    sh['ln4g_c'] = C(np.concatenate([park(i['L1_ffn_ln4_g'][l]) for l in range(2)], 1))
    sh['ln4b_c'] = C(np.concatenate([park(i['L1_ffn_ln4_b'][l]) for l in range(2)], 1))
    pbp = np.concatenate([park(i['L1_ffn_path_b'][l, p])
                          for l in range(2) for p in range(2)], 1)
    sh['pathb_c'] = C(pbp)
    php = np.concatenate([park(i['L1_ffn_phase'][l, p])
                          for l in range(2) for p in range(2)], 1)
    sh['phase_c'] = C(php)
    sh['phase_r'] = C(i['L1_ffn_phase'].reshape(4, D))

    sh['iden'] = np.eye(128, dtype=np.float32)
    sp = np.zeros((128, 128), np.float32)
    for k in range(64):
        sp[2 * k, 2 * k + 1] = 1.0
        sp[2 * k + 1, 2 * k] = 1.0
    sh['sperm'] = sp
    sh['ones_r'] = np.ones((1, 128), np.float32)
    sh['ones_c'] = np.ones((128, 1), np.float32)
    m = np.zeros((128, 128), np.float32)
    m[np.triu_indices(128, 1)] = NEG_INF
    sh['maskd'] = m
    freqs = 1.0 / 10000.0 ** (np.arange(0, ROPE, 2, dtype=np.float32) / ROPE)
    ang = np.outer(np.arange(S, dtype=np.float32), freqs)
    cos = np.cos(ang).astype(np.float32); sin = np.sin(ang).astype(np.float32)
    cfm = np.zeros((128, S), np.float32); sfm = np.zeros((128, S), np.float32)
    for r in range(128):
        fi = (r % 64) // 2
        cfm[r] = cos[:, fi]
        sfm[r] = sin[:, fi] * (1.0 if r % 2 else -1.0)
    sh['rope_c'] = cfm
    sh['rope_s'] = sfm

    percore = []
    ncore = 8
    for c in range(ncore):
        xc = i['L0_attn_x'][c * NSEQ:(c + 1) * NSEQ]
        percore.append({'x_in': C(xc.transpose(0, 2, 1))})
    return sh, percore



class Sem:
    __slots__ = ('h', 'n', 'name')
    def __init__(self, h, name):
        self.h = h; self.n = 0; self.name = name
    def bump(self, k):
        self.n += k
        return self.n


class Stage:
    def __init__(self, nc, name):
        self.nc = nc
        self.name = name
        self.q = {e: [] for e in ENGINES}

    def op(self, eng, method, *args, wait=(), inc=None, **kw):
        post = inc[0].bump(inc[1]) if inc is not None else None
        def fn(e, args=args, kw=kw, wait=tuple(wait), inc=inc):
            for s, v in wait:
                e.wait_ge(s.h, v)
            ins = getattr(e, method)(*args, **kw)
            if inc is not None:
                ins.then_inc(inc[0].h, inc[1])
        self.q[eng].append(fn)
        return post

    def wait(self, eng, sem, val):
        self.q[eng].append(lambda e, s=sem, v=val: e.wait_ge(s.h, v))

    def dma(self, eng, out, in_, sem, wait=()):
        return self.op(eng, 'dma_start', out, in_, wait=wait, inc=(sem, 16))

    def realize(self):
        with self.nc.Block(self.name) as blk:
            for eng in ENGINES:
                fns = self.q[eng]
                if not fns:
                    continue
                def body(e, fns=fns):
                    for f in fns:
                        f(e)
                getattr(blk, eng)(body)


class PB:
    def __init__(self):
        self.nc = bass.Bass('TRN2', target_bir_lowering=False)
        self.ctx = ExitStack()
        self.sems = {}
        self.dram = {}
        self.sb = {}

    def sem(self, name):
        if name not in self.sems:
            h = self.ctx.enter_context(self.nc.semaphore(name))
            self.sems[name] = Sem(h, name)
        return self.sems[name]

    def fresh_engine_sems(self, tag):
        for nm in ('mm', 'ev_v', 'ev_s'):
            h = self.ctx.enter_context(self.nc.semaphore(f'{nm}_{tag}'))
            self.sems[nm] = Sem(h, f'{nm}_{tag}')

    def dt(self, name, shape, dtype, kind='Internal'):
        t = self.nc.dram_tensor(name, list(shape), dtype, kind=kind)
        self.dram[name] = t
        self.kinds = getattr(self, 'kinds', {})
        self.kinds[name] = kind
        return t

    def sbuf(self, name, shape, dtype):
        t = self.ctx.enter_context(self.nc.sbuf_tensor(name, list(shape), dtype))
        self.sb[name] = t
        return t

    def register_consts(self, values):
        self.const_vals = []
        for i, v in enumerate(sorted(set(float(x) for x in values))):
            t = self.sbuf(f'constf_{i}', [128, 1], F32)
            self.nc.const_aps.register(F32, v, t[:, 0:1])
            self.const_vals.append((t, v))



def emit_ln(pb, st, L, xg, xhat_bf, eps=EPS_LN, wait_x=(), wait_xhat=(), gb=None):
    mm, ev_v, ev_s = pb.sem('mm'), pb.sem('ev_v'), pb.sem('ev_s')
    onesc, onesr = pb.sb['onesc_bf'], pb.sb['onesr_f']
    xc = [L['xc0'], L['xc1']]; sq = [L['sq0'], L['sq1']]
    rows = L['rows']
    m_row = rows[:, 0:512]; var_row = rows[:, 512:1024]
    rstd_row = rows[:, 1024:1536]; mr_row = rows[:, 1536:2048]
    for c in range(16):
        b = c % 2
        w_v = [(mm, mm.n)] if c >= 2 else []
        st.op('vector', 'tensor_copy', xc[b][:, :], xg[:, c, :],
              wait=list(wait_x) + w_v, inc=(ev_v, 1))
        st.op('scalar', 'activation', sq[b][:, :], xg[:, c, :], AF.Square,
              wait=list(wait_x) + w_v, inc=(ev_s, 1))
        st.op('tensor', 'matmul', L['pS1'][:, :], onesc[:, :], xc[b][:, :],
              start=(c == 0), stop=(c == 15), wait=[(ev_v, ev_v.n)])
        st.op('tensor', 'matmul', L['pS2'][:, :], onesc[:, :], sq[b][:, :],
              start=(c == 0), stop=(c == 15), wait=[(ev_s, ev_s.n)], inc=(mm, 1))
    st.op('vector', 'tensor_scalar_mul', m_row, L['pS1'][:, :], 1.0 / D,
          wait=[(mm, mm.n)], inc=(ev_v, 1))
    st.op('vector', 'tensor_scalar_mul', var_row, L['pS2'][:, :], 1.0 / D,
          wait=[(mm, mm.n)], inc=(ev_v, 1))
    st.op('vector', 'tensor_mul', mr_row, m_row, m_row, wait=[(ev_v, ev_v.n)],
          inc=(ev_v, 1))
    st.op('vector', 'tensor_sub', var_row, var_row, mr_row, wait=[(ev_v, ev_v.n)],
          inc=(ev_v, 1))
    st.op('scalar', 'activation', rstd_row, var_row, AF.Ln, bias=float(eps),
          scale=1.0, wait=[(ev_v, ev_v.n)], inc=(ev_s, 1))
    st.op('scalar', 'activation', rstd_row, rstd_row, AF.Exp, bias=0.0, scale=-0.5,
          wait=[(ev_s, ev_s.n)], inc=(ev_s, 1))
    st.op('vector', 'tensor_mul', mr_row, m_row, rstd_row, wait=[(ev_s, ev_s.n)],
          inc=(ev_v, 1))
    st.op('tensor', 'matmul', L['pB1'][:, :], onesr[:, :], rstd_row,
          start=True, stop=True, wait=[(ev_s, ev_s.n), (ev_v, ev_v.n)], inc=(mm, 1))
    st.op('tensor', 'matmul', L['pB2'][:, :], onesr[:, :], mr_row,
          start=True, stop=True, wait=[(ev_v, ev_v.n)], inc=(mm, 1))
    bc_done = mm.n
    tmp = [L['tmp0'], L['tmp1']]
    for c in range(16):
        b = c % 2
        w = [(mm, bc_done), (ev_v, ev_v.n)]
        st.op('vector', 'tensor_mul', tmp[b][:, :], xg[:, c, :], L['pB1'][:, :],
              wait=w, inc=(ev_v, 1))
        if gb is None:
            st.op('vector', 'tensor_sub', xhat_bf[:, c, :], tmp[b][:, :],
                  L['pB2'][:, :], wait=[(ev_v, ev_v.n)] + list(wait_xhat),
                  inc=(ev_v, 1))
        else:
            gt, bt, off = gb
            st.op('vector', 'tensor_sub', tmp[b][:, :], tmp[b][:, :],
                  L['pB2'][:, :], wait=[(ev_v, ev_v.n)], inc=(ev_v, 1))
            st.op('vector', 'tensor_scalar', xhat_bf[:, c, :], tmp[b][:, :],
                  gt[:, off + c:off + c + 1], bt[:, off + c:off + c + 1],
                  ALU.mult, ALU.add, wait=[(ev_v, ev_v.n)] + list(wait_xhat),
                  inc=(ev_v, 1))


def alloc_ln(pb, c, tag):
    nc = pb.nc
    L = {}
    L['xc0'] = c.enter_context(nc.sbuf_tensor(f'ln_xc0_{tag}', [128, 512], BF16))
    L['xc1'] = c.enter_context(nc.sbuf_tensor(f'ln_xc1_{tag}', [128, 512], BF16))
    L['sq0'] = c.enter_context(nc.sbuf_tensor(f'ln_sq0_{tag}', [128, 512], BF16))
    L['sq1'] = c.enter_context(nc.sbuf_tensor(f'ln_sq1_{tag}', [128, 512], BF16))
    L['rows'] = c.enter_context(nc.sbuf_tensor(f'ln_rows_{tag}', [1, 2048], F32))
    L['tmp0'] = c.enter_context(nc.sbuf_tensor(f'ln_tmp0_{tag}', [128, 512], F32))
    L['tmp1'] = c.enter_context(nc.sbuf_tensor(f'ln_tmp1_{tag}', [128, 512], F32))
    L['pS1'] = c.enter_context(nc.psum_tensor(f'ln_pS1_{tag}', [1, 512], F32))
    L['pS2'] = c.enter_context(nc.psum_tensor(f'ln_pS2_{tag}', [1, 512], F32))
    L['pB1'] = c.enter_context(nc.psum_tensor(f'ln_pB1_{tag}', [128, 512], F32))
    L['pB2'] = c.enter_context(nc.psum_tensor(f'ln_pB2_{tag}', [128, 512], F32))
    return L


def fm3(dram_ap, p=128):
    return dram_ap.rearrange('(c p) t -> p c t', p=p)



def stage_init(pb, casts):
    nc = pb.nc
    st = Stage(nc, 'init')
    pb.fresh_engine_sems('init')
    go, ld = pb.sem('go'), pb.sem('ld')
    d = pb.dram
    pb.sbuf('iden_bf', [128, 128], BF16)
    pb.sbuf('sperm_bf', [128, 128], BF16)
    pb.sbuf('onesc_bf', [128, 1], BF16)
    pb.sbuf('onesr_f', [1, 128], F32)
    pb.sbuf('maskd_f', [128, 128], F32)
    pb.sbuf('kvw_bc', [128, KVR], F32)
    for nm in ('bq_c', 'bk_c', 'bo_c', 'ampb_c', 'phb_c'):
        pb.sbuf(nm + '_sb', [128, 16], F32)
    pb.sbuf('entb_c_sb', [128, 128], F32)
    pb.sbuf('entpb_c_sb', [128, 32], F32)
    for nm in ('ln3g_c', 'ln3b_c', 'ln4g_c', 'ln4b_c'):
        pb.sbuf(nm + '_sb', [128, 32], F32)
    pb.sbuf('pathb_c_sb', [128, 64], F32)
    pb.sbuf('phase_c_sb', [128, 64], F32)
    pb.sbuf('bcomb_c_sb', [128, 32], F32)
    pb.sbuf('kvw_row_sb', [1, KVR], F32)
    st.op('gpsimd', 'memset', pb.sb['onesc_bf'][:, :], 1.0)
    st.op('gpsimd', 'memset', pb.sb['onesr_f'][:, :], 1.0)
    for t, v in pb.const_vals:
        st.op('gpsimd', 'memset', t[:, :], v)
    parknames = ['bq_c', 'bk_c', 'bo_c', 'ampb_c', 'phb_c', 'entb_c', 'entpb_c',
                 'ln3g_c', 'ln3b_c', 'ln4g_c', 'ln4b_c', 'pathb_c', 'phase_c',
                 'kvw_row']
    pairs = [('iden', 'iden_bf'), ('sperm', 'sperm_bf'), ('maskd', 'maskd_f')] + \
            [(nm, nm + '_sb') for nm in parknames]
    for nm, dst in pairs:
        st.dma('gpsimd', pb.sb[dst][:, :], d[nm][:, :], go)
    consts_done = (go, go.n)
    mm, ev_v = pb.sem('mm'), pb.sem('ev_v')
    with nc.psum_tensor('init_pbc', [128, KVR], F32) as pbc:
        st.op('tensor', 'matmul', pbc[:, :], pb.sb['onesr_f'][:, :],
              pb.sb['kvw_row_sb'][:, :], start=True, stop=True,
              wait=[consts_done], inc=(mm, 1))
        st.op('vector', 'tensor_copy', pb.sb['kvw_bc'][:, :], pbc[:, :],
              wait=[(mm, mm.n)], inc=(ev_v, 1))
        pb.cast_grp = {}
        for srcn, dstn, grp in casts:
            gsem = pb.sem('c_' + grp)
            st_ = pb.dram[srcn]
            shp = list(st_.shape)
            if len(shp) == 3:
                for j in range(shp[0]):
                    st.dma('gpsimd', pb.dram[dstn][j], st_[j], gsem)
            else:
                for r0 in range(0, shp[0], 512):
                    st.dma('gpsimd', pb.dram[dstn][r0:r0 + 512, :],
                           st_[r0:r0 + 512, :], gsem)
            pb.cast_grp[grp] = (gsem, gsem.n)
        st.realize()
    return st



def stage_l0a1(pb, s):
    nc = pb.nc
    st = Stage(nc, f'l0a1_s{s}')
    pb.fresh_engine_sems(f'l0a1_s{s}')
    d = pb.dram
    mm, ev_v, ev_s = pb.sem('mm'), pb.sem('ev_v'), pb.sem('ev_s')
    ld = pb.sem('dX')
    gw = pb.sem('dW')
    gox = pb.sem('dOx')
    goQ = [pb.sem('dO0'), pb.sem('dO1')]
    with ExitStack() as c:
        wqn = c.enter_context(nc.sbuf_tensor(f'a1_wqn_{s}', [128, 16, 2048], BF16))
        xg = c.enter_context(nc.sbuf_tensor(f'a1_xg_{s}', [128, 16, 512], F32))
        xhat = c.enter_context(nc.sbuf_tensor(f'a1_xhat_{s}', [128, 16, 512], BF16))
        qev = [c.enter_context(nc.sbuf_tensor(f'a1_qev{b}_{s}', [128, 512], BF16))
               for b in range(2)]
        pq = [c.enter_context(nc.psum_tensor(f'a1_pq{b}_{s}', [128, 512], F32))
              for b in range(2)]
        L = alloc_ln(pb, c, f'a1_{s}')
        for k in range(16):
            st.dma('sync', wqn[:, k, :], fm3(d['wqn_bf'])[:, k, :], gw,
                   wait=[pb.cast_grp['l0']] if k == 0 else ())
        wq_ready = gw.n
        xin_s = fm3(d['x_in'][s])
        xhs = fm3(d['xhat_scr'][s])
        qns = fm3(d['q_nope_scr'][s])
        evac_done = {0: (ev_s, 0), 1: (ev_s, 0)}
        store_done = {0: (goQ[0], goQ[0].n), 1: (goQ[1], goQ[1].n)}
        for g in range(4):
            t0 = g * GSZ
            w = [(ev_v, ev_v.n), (ev_s, ev_s.n), (mm, mm.n)] if g else []
            xld = st.dma('sync', xg[:, :, :], xin_s[:, :, t0:t0 + GSZ], ld, wait=w)
            wxh = [(mm, mm.n), (gox, gox.n)] if g else []
            emit_ln(pb, st, L, xg, xhat, wait_x=[(ld, xld)], wait_xhat=wxh)
            xhat_ready = ev_v.n
            st.dma('scalar', xhs[:, :, t0:t0 + GSZ], xhat[:, :, :], gox,
                   wait=[(ev_v, xhat_ready)])
            for m in range(16):
                b = m % 2
                for k in range(16):
                    w = [(ev_v, xhat_ready)]
                    if k == 0:
                        w.append(evac_done[b])
                        if m == 0 and g == 0:
                            w.append((gw, wq_ready))
                    st.op('tensor', 'matmul', pq[b][:, :], wqn[:, k, m * 128:(m + 1) * 128],
                          xhat[:, k, :], start=(k == 0), stop=(k == 15), wait=w,
                          inc=(mm, 1) if k == 15 else None)
                st.op('scalar', 'copy', qev[b][:, :], pq[b][:, :],
                      wait=[(mm, mm.n), store_done[b]], inc=(ev_s, 1))
                evac_done[b] = (ev_s, ev_s.n)
                st.dma('scalar', qns[:, m, t0:t0 + GSZ], qev[b][:, :], goQ[b],
                       wait=[(ev_s, ev_s.n)])
                store_done[b] = (goQ[b], goQ[b].n)
        for sm in (gox, goQ[0], goQ[1]):
            st.wait('scalar', sm, sm.n)
        st.realize()



def stage_l0a2(pb, s, kvb):
    nc = pb.nc
    st = Stage(nc, f'l0a2_s{s}')
    pb.fresh_engine_sems(f'l0a2_s{s}')
    d = pb.dram
    mm, ev_v, ev_s = pb.sem('mm'), pb.sem('ev_v'), pb.sem('ev_s')
    ld = pb.sem('dX')
    gw = pb.sem('dW')
    goQ = [pb.sem('dO0'), pb.sem('dO1')]
    iden = pb.sb['iden_bf']; sperm = pb.sb['sperm_bf']; kvw_bc = pb.sb['kvw_bc']
    with ExitStack() as c:
        wqp = c.enter_context(nc.sbuf_tensor(f'a2_wqp_{s}', [128, 16, 1024], BF16))
        wkva = c.enter_context(nc.sbuf_tensor(f'a2_wkva_{s}', [128, 16, 576], BF16))
        ropc = c.enter_context(nc.sbuf_tensor(f'a2_ropc_{s}', [128, 2048], F32))
        rops = c.enter_context(nc.sbuf_tensor(f'a2_rops_{s}', [128, 2048], F32))
        xhat = c.enter_context(nc.sbuf_tensor(f'a2_xhat_{s}', [128, 16, 512], BF16))
        qpe_ev = [c.enter_context(nc.sbuf_tensor(f'a2_qpe{b}_{s}', [128, 512], BF16))
                  for b in range(2)]
        qpe_ro = [c.enter_context(nc.sbuf_tensor(f'a2_qro{b}_{s}', [128, 512], BF16))
                  for b in range(2)]
        t1 = c.enter_context(nc.sbuf_tensor(f'a2_t1_{s}', [128, 512], F32))
        t2 = c.enter_context(nc.sbuf_tensor(f'a2_t2_{s}', [128, 512], F32))
        trash = c.enter_context(nc.sbuf_tensor(f'a2_trash_{s}', [128, 512], BF16))
        ss = c.enter_context(nc.sbuf_tensor(f'a2_ss_{s}', [128, 2], F32))
        kvt = c.enter_context(nc.sbuf_tensor(f'a2_kvt_{s}', [128, 512], F32))
        kpe_ev = c.enter_context(nc.sbuf_tensor(f'a2_kpeev_{s}', [128, 64], BF16))
        pq = [c.enter_context(nc.psum_tensor(f'a2_pq{b}_{s}', [128, 512], F32))
              for b in range(2)]
        psw = c.enter_context(nc.psum_tensor(f'a2_psw_{s}', [128, 512], F32))
        pkv = c.enter_context(nc.psum_tensor(f'a2_pkv_{s}', [128, 512], F32))
        pkpe = c.enter_context(nc.psum_tensor(f'a2_pkpe_{s}', [128, 64], F32))
        pt = [c.enter_context(nc.psum_tensor(f'a2_pt{b}_{s}', [128, 128], BF16))
              for b in range(2)]
        w0 = [pb.cast_grp['l0']]
        for k in range(16):
            st.dma('sync', wqp[:, k, :], fm3(d['wqp_bf'])[:, k, :], gw,
                   wait=w0 if k == 0 else ())
            st.dma('sync', wkva[:, k, :], fm3(d['wkva_bf'])[:, k, :], gw)
        st.dma('sync', ropc[:, :], d['rope_c'][:, :], gw)
        st.dma('sync', rops[:, :], d['rope_s'][:, :], gw)
        w_ready = gw.n
        rope_ready = gw.n
        xhs = fm3(d['xhat_scr'][s])
        qps = fm3(d['q_pe_scr'][s])
        evac_done = {0: (ev_s, 0), 1: (ev_s, 0)}
        rope_done = {0: (ev_v, 0), 1: (ev_v, 0)}
        st_a2 = {0: (goQ[0], goQ[0].n), 1: (goQ[1], goQ[1].n)}
        for g in range(4):
            t0 = g * GSZ
            w = [(mm, mm.n), (ev_s, ev_s.n)] if g else []
            xld = st.dma('sync', xhat[:, :, :], xhs[:, :, t0:t0 + GSZ], ld, wait=w)
            for m in range(8):
                b = m % 2
                for k in range(16):
                    w = [(ld, xld)]
                    if k == 0:
                        w.append(evac_done[b])
                        if m == 0 and g == 0:
                            w.append((gw, w_ready))
                    st.op('tensor', 'matmul', pq[b][:, :],
                          wqp[:, k, m * 128:(m + 1) * 128], xhat[:, k, :],
                          start=(k == 0), stop=(k == 15), wait=w,
                          inc=(mm, 1) if k == 15 else None)
                st.op('scalar', 'copy', qpe_ev[b][:, :], pq[b][:, :],
                      wait=[(mm, mm.n), rope_done[b]], inc=(ev_s, 1))
                evac_done[b] = (ev_s, ev_s.n)
                st.op('tensor', 'matmul', psw[:, :], sperm[:, :], qpe_ev[b][:, :],
                      start=True, stop=True,
                      wait=[(ev_s, ev_s.n), (ev_v, ev_v.n)], inc=(mm, 1))
                st.op('vector', 'tensor_mul', t1[:, :], qpe_ev[b][:, :],
                      ropc[:, t0:t0 + GSZ], wait=[(ev_s, ev_s.n), (gw, rope_ready),
                                                  (ev_v, ev_v.n)], inc=(ev_v, 1))
                st.op('vector', 'tensor_mul', t2[:, :], psw[:, :],
                      rops[:, t0:t0 + GSZ], wait=[(mm, mm.n), (ev_v, ev_v.n)],
                      inc=(ev_v, 1))
                st.op('vector', 'tensor_add', qpe_ro[b][:, :], t1[:, :], t2[:, :],
                      wait=[(ev_v, ev_v.n), st_a2[b]], inc=(ev_v, 1))
                rope_done[b] = (ev_v, ev_v.n)
                st.dma('scalar', qps[:, m, t0:t0 + GSZ], qpe_ro[b][:, :], goQ[b],
                       wait=[(ev_v, ev_v.n)])
                st_a2[b] = (goQ[b], goQ[b].n)
            for tc in range(4):
                ti = g * 4 + tc
                for k in range(16):
                    w = [(ld, xld)]
                    if k == 0:
                        w.append((ev_v, ev_v.n))
                    st.op('tensor', 'matmul', pkv[:, :], xhat[:, k, tc * 128:(tc + 1) * 128],
                          wkva[:, k, 0:512], start=(k == 0), stop=(k == 15), wait=w,
                          inc=(mm, 1) if k == 15 else None)
                    st.op('tensor', 'matmul', pkpe[:, :], xhat[:, k, tc * 128:(tc + 1) * 128],
                          wkva[:, k, 512:576], start=(k == 0), stop=(k == 15),
                          inc=(mm, 1) if k == 15 else None)
                mm_kv = mm.n
                st.op('scalar', 'activation', trash[:, :], pkv[:, :], AF.Square,
                      accum_out=ss[:, 0:1], wait=[(mm, mm_kv)], inc=(ev_s, 1))
                st.op('scalar', 'activation', ss[:, 1:2], ss[:, 0:1], AF.Ln,
                      bias=EPS_RMS, scale=1.0 / KVR, wait=[(ev_s, ev_s.n)],
                      inc=(ev_s, 1))
                st.op('scalar', 'activation', ss[:, 1:2], ss[:, 1:2], AF.Exp,
                      bias=0.0, scale=-0.5, wait=[(ev_s, ev_s.n)], inc=(ev_s, 1))
                st.op('vector', 'tensor_scalar_mul', kvt[:, :], pkv[:, :], ss[:, 1:2],
                      wait=[(ev_s, ev_s.n), (ev_v, ev_v.n)], inc=(ev_v, 1))
                st.op('vector', 'tensor_mul', kvb['kv_tm'][:, ti, :], kvt[:, :],
                      kvw_bc[:, :], wait=[(ev_v, ev_v.n), (mm, mm.n)], inc=(ev_v, 1))
                st.op('vector', 'tensor_copy', kpe_ev[:, :], pkpe[:, :],
                      wait=[(mm, mm.n), (ev_v, ev_v.n)], inc=(ev_v, 1))
                kv_ready = ev_v.n
                for c4 in range(4):
                    b = c4 % 2
                    st.op('tensor', 'transpose', pt[b][:, :],
                          kvb['kv_tm'][:, ti, c4 * 128:(c4 + 1) * 128], iden[:, :],
                          wait=[(ev_v, ev_v.n)], inc=(mm, 1))
                    st.op('vector', 'tensor_copy',
                          kvb['kv_fm'][:, c4, ti * 128:(ti + 1) * 128], pt[b][:, :],
                          wait=[(mm, mm.n)], inc=(ev_v, 1))
                st.op('tensor', 'transpose', pt[0][0:64, :], kpe_ev[:, 0:64],
                      iden[:, :], wait=[(ev_v, ev_v.n)], inc=(mm, 1))
                st.op('vector', 'tensor_copy',
                      kvb['kpe_pre'][0:64, ti * 128:(ti + 1) * 128], pt[0][0:64, :],
                      wait=[(mm, mm.n)], inc=(ev_v, 1))
            st.op('tensor', 'matmul', psw[0:64, :], sperm[0:64, 0:64],
                  kvb['kpe_pre'][0:64, t0:t0 + GSZ], start=True, stop=True,
                  wait=[(ev_v, ev_v.n)], inc=(mm, 1))
            st.op('vector', 'tensor_mul', t1[0:64, :], kvb['kpe_pre'][0:64, t0:t0 + GSZ],
                  ropc[0:64, t0:t0 + GSZ], wait=[(ev_v, ev_v.n)], inc=(ev_v, 1))
            st.op('vector', 'tensor_mul', t2[0:64, :], psw[0:64, :],
                  rops[0:64, t0:t0 + GSZ], wait=[(mm, mm.n), (ev_v, ev_v.n)],
                  inc=(ev_v, 1))
            st.op('vector', 'tensor_add', kvb['kpe_fm'][0:64, t0:t0 + GSZ],
                  t1[0:64, :], t2[0:64, :], wait=[(ev_v, ev_v.n)], inc=(ev_v, 1))
        for sm in (goQ[0], goQ[1]):
            st.wait('scalar', sm, sm.n)
        st.realize()



def stage_l0b(pb, s, kvb):
    nc = pb.nc
    st = Stage(nc, f'l0b_s{s}')
    pb.fresh_engine_sems(f'l0b_s{s}')
    d = pb.dram
    mm, ev_v, ev_s = pb.sem('mm'), pb.sem('ev_v'), pb.sem('ev_s')
    gw = pb.sem('dW')
    ldQ = [pb.sem('dQ0'), pb.sem('dQ1')]
    ldR = [pb.sem('dR0'), pb.sem('dR1')]
    goQ = [pb.sem('dO0'), pb.sem('dO1')]
    iden = pb.sb['iden_bf']; maskd = pb.sb['maskd_f']
    with ExitStack() as c:
        wos = [c.enter_context(nc.sbuf_tensor(f'b_wos{b}_{s}', [128, 16, 512], BF16))
               for b in range(2)]
        w1 = c.enter_context(nc.sbuf_tensor(f'b_w1_{s}', [128, 16, 512], BF16))
        w2t = c.enter_context(nc.sbuf_tensor(f'b_w2t_{s}', [128, 64, 128], BF16))
        qn = [c.enter_context(nc.sbuf_tensor(f'b_qn{b}_{s}', [128, 16, 128], BF16))
              for b in range(2)]
        qp = [c.enter_context(nc.sbuf_tensor(f'b_qp{b}_{s}', [64, 16, 128], BF16))
              for b in range(2)]
        qlb = [c.enter_context(nc.sbuf_tensor(f'b_qlb{p}_{s}', [128, 512], BF16))
               for p in range(3)]
        ssb = [c.enter_context(nc.sbuf_tensor(f'b_ssb{p}_{s}', [128, 2048], F32))
               for p in range(3)]
        ebf = [c.enter_context(nc.sbuf_tensor(f'b_ebf{p}_{s}', [128, 2048], BF16))
               for p in range(3)]
        etb = [c.enter_context(nc.sbuf_tensor(f'b_etb{p}_{s}', [128, 16, 128], BF16))
               for p in range(2)]
        olat = [c.enter_context(nc.sbuf_tensor(f'b_olat{p}_{s}', [128, 512], BF16))
                for p in range(2)]
        olatT = [c.enter_context(nc.sbuf_tensor(f'b_olatT{p}_{s}', [128, 4, 128], BF16))
                 for p in range(2)]
        attn = c.enter_context(nc.sbuf_tensor(f'b_attn_{s}', [128, 16, 512], BF16))
        xold = [c.enter_context(nc.sbuf_tensor(f'b_xold{b}_{s}', [128, 512], F32))
                for b in range(2)]
        x1t = [c.enter_context(nc.sbuf_tensor(f'b_x1t{b}_{s}', [128, 512], F32))
               for b in range(2)]
        rowsb = [c.enter_context(nc.sbuf_tensor(f'b_rows{p}_{s}', [128, 4], F32))
                 for p in range(3)]
        pql = c.enter_context(nc.psum_tensor(f'b_pql_{s}', [128, 512], F32))
        pat = c.enter_context(nc.psum_tensor(f'b_pat_{s}', [128, 512], F32))
        psc = [c.enter_context(nc.psum_tensor(f'b_psc{b}_{s}', [128, 512], F32))
               for b in range(2)]
        pet = [c.enter_context(nc.psum_tensor(f'b_pet{b}_{s}', [128, 128], BF16))
               for b in range(2)]
        pol = [c.enter_context(nc.psum_tensor(f'b_pol{p}_{s}', [128, 512], F32))
               for p in range(2)]
        pwo = [pql, pat]
        ldS = [pb.sem('dS0'), pb.sem('dS1')]
        w0 = [pb.cast_grp['l0']]
        for k in range(16):
            st.dma('sync', w1[:, k, :], d['w1_bf'][k, :, :], gw,
                   wait=w0 if k == 0 else ())
            st.dma('sync', w2t[:, k * 4:(k + 1) * 4, :], fm3(d['w2t_bf'][k]), gw)
        w_ready = gw.n
        wos_used = {0: (mm, 0), 1: (mm, 0)}
        qns = fm3(d['q_nope_scr'][s])
        qps = d['q_pe_scr'][s].rearrange('(h p) t -> p h t', p=64)
        xin_s = fm3(d['x_in'][s]); xr = fm3(d['xr0'][s])
        q_used = {0: (mm, 0), 1: (mm, 0)}
        q_loaded = {}
        sc_ev = {i: (ev_s, 0) for i in range(2)}
        et_ev = {0: (ev_v, 0), 1: (ev_v, 0)}
        ql_used = {i: (mm, 0) for i in range(3)}
        ssb_used = {i: (ev_s, 0) for i in range(3)}
        e_used = {i: (mm, 0) for i in range(3)}
        etb_used = {0: (mm, 0), 1: (mm, 0)}
        pol_ev = {0: (ev_s, 0), 1: (ev_s, 0)}
        ol_used = {0: (mm, 0), 1: (mm, 0)}
        olt_used = {0: (mm, 0), 1: (mm, 0)}
        pql_ev = {0: (ev_s, 0), 1: (ev_s, 0)}
        x1_st = {0: (goQ[0], goQ[0].n), 1: (goQ[1], goQ[1].n)}
        xo_used = {0: (ev_v, 0), 1: (ev_v, 0)}
        kbi = 0
        for qt in range(16):
            b = qt % 2
            w = [q_used[b]]
            if qt == 0:
                w.append((gw, w_ready))
            st.dma('sync', qn[b][:, :, :], qns[:, :, qt * 128:(qt + 1) * 128],
                   ldQ[b], wait=w)
            l2 = st.dma('sync', qp[b][:, :, :], qps[:, :, qt * 128:(qt + 1) * 128],
                        ldQ[b])
            q_loaded[qt] = (ldQ[b], l2)
            v = qt + 1
            kgs = (v + 3) // 4
            state = {}
            sdone = {}

            def p1_pe(h):
                p = h % 3
                for cc in range(4):
                    w = [q_loaded[qt]]
                    if cc == 0:
                        w.append(pql_ev[0])
                    st.op('tensor', 'matmul', pql[:, cc * 128:(cc + 1) * 128],
                          w1[:, h, cc * 128:(cc + 1) * 128], qn[b][:, h, :],
                          start=True, stop=True, wait=w,
                          inc=(mm, 1) if cc == 3 else None)
                st.op('scalar', 'copy', qlb[p][:, :], pql[:, :],
                      wait=[(mm, mm.n), (ev_s, ev_s.n), ql_used[p]], inc=(ev_s, 1))
                pql_ev[0] = (ev_s, ev_s.n)
                ql_ready = (ev_s, ev_s.n)
                nonlocal kbi
                kglist = []
                for kg in range(kgs):
                    kb = kbi % 2
                    kbi += 1
                    wdt = min(4, v - kg * 4) * 128
                    for cc in range(4):
                        w = [ql_ready, sc_ev[kb]] if cc == 0 else []
                        st.op('tensor', 'matmul', psc[kb][:, 0:wdt],
                              qlb[p][:, cc * 128:(cc + 1) * 128],
                              kvb['kv_fm'][:, cc, kg * 512:kg * 512 + wdt],
                              start=(cc == 0), stop=False, wait=w)
                    st.op('tensor', 'matmul', psc[kb][:, 0:wdt],
                          qp[b][:, h, :],
                          kvb['kpe_fm'][0:64, kg * 512:kg * 512 + wdt],
                          start=False, stop=True, inc=(mm, 1))
                    if kgs >= 3:
                        if kg == kgs - 1:
                            dcol = wdt - 128
                            st.op('vector', 'tensor_add',
                                  psc[kb][:, dcol:dcol + 128],
                                  psc[kb][:, dcol:dcol + 128], maskd[:, :],
                                  wait=[(mm, mm.n), (ev_v, ev_v.n)], inc=(ev_v, 1))
                            st.op('scalar', 'copy',
                                  ssb[p][:, kg * 512:kg * 512 + wdt],
                                  psc[kb][:, 0:wdt],
                                  wait=[(ev_v, ev_v.n), ssb_used[p], (ev_s, ev_s.n)],
                                  inc=(ev_s, 1))
                        else:
                            st.op('scalar', 'copy',
                                  ssb[p][:, kg * 512:kg * 512 + wdt],
                                  psc[kb][:, 0:wdt],
                                  wait=[(mm, mm.n), ssb_used[p], (ev_s, ev_s.n)],
                                  inc=(ev_s, 1))
                        sc_ev[kb] = (ev_s, ev_s.n)
                    else:
                        kglist.append((kg, kb, wdt, mm.n))
                ql_used[p] = (mm, mm.n)
                sdone[h] = kglist

            def p1_rest(h):
                p = h % 3
                negmax = rowsb[p][:, 0:1]; negmaxs = rowsb[p][:, 1:2]
                rs = rowsb[p][:, 2:3]
                for kg, kb, wdt, mmv in sdone.pop(h):
                    if kg == kgs - 1:
                        dcol = wdt - 128
                        st.op('vector', 'tensor_add', psc[kb][:, dcol:dcol + 128],
                              psc[kb][:, dcol:dcol + 128], maskd[:, :],
                              wait=[(mm, mmv), (ev_v, ev_v.n)], inc=(ev_v, 1))
                        st.op('scalar', 'copy', ssb[p][:, kg * 512:kg * 512 + wdt],
                              psc[kb][:, 0:wdt],
                              wait=[(ev_v, ev_v.n), ssb_used[p], (ev_s, ev_s.n)],
                              inc=(ev_s, 1))
                    else:
                        st.op('scalar', 'copy', ssb[p][:, kg * 512:kg * 512 + wdt],
                              psc[kb][:, 0:wdt],
                              wait=[(mm, mmv), ssb_used[p], (ev_s, ev_s.n)],
                              inc=(ev_s, 1))
                    sc_ev[kb] = (ev_s, ev_s.n)
                st.op('vector', 'reduce_max', negmax, ssb[p][:, 0:v * 128], axis=AX.X,
                      negate=True, wait=[(ev_s, ev_s.n), (ev_v, ev_v.n)], inc=(ev_v, 1))
                st.op('vector', 'tensor_scalar_mul', negmaxs, negmax, float(L0_SCALE),
                      wait=[(ev_v, ev_v.n)], inc=(ev_v, 1))
                st.op('scalar', 'activation', ebf[p][:, 0:v * 128], ssb[p][:, 0:v * 128],
                      AF.Exp, bias=negmaxs, scale=float(L0_SCALE), accum_out=rs,
                      wait=[(ev_v, ev_v.n), e_used[p], (ev_s, ev_s.n)], inc=(ev_s, 1))
                ssb_used[p] = (ev_s, ev_s.n)
                st.op('vector', 'reciprocal', rowsb[p][:, 3:4], rs,
                      wait=[(ev_s, ev_s.n), (ev_v, ev_v.n)], inc=(ev_v, 1))
                st.op('vector', 'tensor_scalar_mul', ebf[p][:, 0:v * 128],
                      ebf[p][:, 0:v * 128], rowsb[p][:, 3:4],
                      wait=[(ev_v, ev_v.n)], inc=(ev_v, 1))
                state[h] = (ev_v, ev_v.n)

            def phase2(h):
                p = h % 3
                p2 = h % 2
                e_ready = state.pop(h)
                for kt in range(v):
                    tb = kt % 2
                    st.op('tensor', 'transpose', pet[tb][:, :],
                          ebf[p][:, kt * 128:(kt + 1) * 128], iden[:, :],
                          wait=[e_ready, et_ev[tb]], inc=(mm, 1))
                    st.op('vector', 'tensor_copy', etb[p2][:, kt, :], pet[tb][:, :],
                          wait=[(mm, mm.n), etb_used[p2] if kt == 0 else (ev_v, 0)],
                          inc=(ev_v, 1))
                    et_ev[tb] = (ev_v, ev_v.n)
                e_used[p] = (mm, mm.n)
                for kt in range(v):
                    w = [(ev_v, ev_v.n), pol_ev[p2]] if kt == 0 else []
                    st.op('tensor', 'matmul', pol[p2][:, :], etb[p2][:, kt, :],
                          kvb['kv_tm'][:, kt, :], start=(kt == 0), stop=(kt == v - 1),
                          wait=w, inc=(mm, 1) if kt == v - 1 else None)
                etb_used[p2] = (mm, mm.n)
                st.op('scalar', 'copy', olat[p2][:, :], pol[p2][:, :],
                      wait=[(mm, mm.n), ol_used[p2], (ev_s, ev_s.n)], inc=(ev_s, 1))
                pol_ev[p2] = (ev_s, ev_s.n)
                for cc in range(4):
                    tb = cc % 2
                    st.op('tensor', 'transpose', pet[tb][:, :],
                          olat[p2][:, cc * 128:(cc + 1) * 128], iden[:, :],
                          wait=[(ev_s, ev_s.n), et_ev[tb]], inc=(mm, 1))
                    st.op('vector', 'tensor_copy', olatT[p2][:, cc, :], pet[tb][:, :],
                          wait=[(mm, mm.n), olt_used[p2] if cc == 0 else (ev_v, 0)],
                          inc=(ev_v, 1))
                    et_ev[tb] = (ev_v, ev_v.n)
                ol_used[p2] = (mm, mm.n)
                for cc in range(4):
                    w = [(ev_v, ev_v.n), pql_ev[1]] if cc == 0 else []
                    st.op('tensor', 'matmul', pat[:, 0:128], w2t[:, h * 4 + cc, :],
                          olatT[p2][:, cc, :], start=(cc == 0), stop=(cc == 3),
                          wait=w, inc=(mm, 1) if cc == 3 else None)
                olt_used[p2] = (mm, mm.n)
                st.op('scalar', 'copy', attn[:, h, (qt % 4) * 128:(qt % 4 + 1) * 128],
                      pat[:, 0:128], wait=[(mm, mm.n), (ev_s, ev_s.n)], inc=(ev_s, 1))
                pql_ev[1] = (ev_s, ev_s.n)

            for h in range(18):
                if h < 16:
                    p1_pe(h)
                if h >= 2:
                    phase2(h - 2)
                if h < 16:
                    p1_rest(h)
            q_used[b] = (mm, mm.n)
            if qt % 4 == 3:
                t0 = (qt - 3) * 128
                attn_ready = (ev_s, ev_s.n)
                for sl in range(4):
                    wb2 = sl % 2
                    wld = st.dma('sync', wos[wb2][:, :, :],
                                 fm3(d['wo_bf'])[:, :, sl * 512:(sl + 1) * 512],
                                 ldS[wb2], wait=[wos_used[wb2]])
                    for mi in range(4):
                        m = sl * 4 + mi
                        wb = m % 2
                        xl = st.dma('sync', xold[wb][:, :], xin_s[:, m, t0:t0 + GSZ],
                                    ldR[wb], wait=[xo_used[wb]])
                        for k in range(16):
                            w = [attn_ready, (ev_v, ev_v.n), (ldS[wb2], wld)] \
                                if k == 0 else []
                            if k == 0 and m < 2:
                                w += [pql_ev[0], pql_ev[1], ql_used[0], ql_used[1],
                                      olt_used[0], olt_used[1]]
                            st.op('tensor', 'matmul', pwo[wb][:, :],
                                  wos[wb2][:, k, mi * 128:(mi + 1) * 128],
                                  attn[:, k, :], start=(k == 0), stop=(k == 15),
                                  wait=w, inc=(mm, 1) if k == 15 else None)
                        st.op('vector', 'tensor_add', x1t[wb][:, :], pwo[wb][:, :],
                              xold[wb][:, :],
                              wait=[(mm, mm.n), (ldR[wb], xl), x1_st[wb],
                                    (ev_v, ev_v.n)], inc=(ev_v, 1))
                        xo_used[wb] = (ev_v, ev_v.n)
                        st.dma('scalar', xr[:, m, t0:t0 + GSZ], x1t[wb][:, :],
                               goQ[wb], wait=[(ev_v, ev_v.n)])
                        x1_st[wb] = (goQ[wb], goQ[wb].n)
                    wos_used[wb2] = (mm, mm.n)
                pql_ev[0] = (ev_v, ev_v.n)
                pql_ev[1] = (ev_v, ev_v.n)
        for sm in (goQ[0], goQ[1]):
            st.wait('scalar', sm, sm.n)
        st.realize()


CONST_INPUTS = [
    ('iden', [128, 128], F32), ('sperm', [128, 128], F32), ('maskd', [128, 128], F32),
    ('bq_c', [128, 16], F32), ('bk_c', [128, 16], F32), ('bo_c', [128, 16], F32),
    ('ampb_c', [128, 16], F32), ('phb_c', [128, 16], F32),
    ('entb_c', [128, 128], F32), ('entpb_c', [128, 32], F32),
    ('ln3g_c', [128, 32], F32), ('ln3b_c', [128, 32], F32),
    ('ln4g_c', [128, 32], F32), ('ln4b_c', [128, 32], F32),
    ('pathb_c', [128, 64], F32), ('phase_c', [128, 64], F32),
    ('kvw_row', [1, KVR], F32),
]

L0_WEIGHTS = [
    ('wqn_t', [D, 2048], F32), ('wqp_t', [D, 1024], F32), ('wkva_t', [D, 576], F32),
    ('w1', [16, 128, 512], F32), ('w2t', [16, 512, 128], F32), ('wo_t', [D, D], F32),
    ('rope_c', [128, S], F32), ('rope_s', [128, S], F32),
]

ALL_CONSTS = [0.0, 1.0, -0.5, 0.5, EPS_LN, EPS_RMS, 1.0 / D, 1.0 / KVR,
              1.0 / HID, float(L0_SCALE), float(L1_SCALE), PI_LO, -PI_LO,
              float(np.pi / 2), float(np.pi), 1.0 / 512.0]


def l0_casts(pb):
    pb.dt('wqn_bf', [D, 2048], BF16)
    pb.dt('wqp_bf', [D, 1024], BF16)
    pb.dt('wkva_bf', [D, 576], BF16)
    pb.dt('w1_bf', [16, 128, 512], BF16)
    pb.dt('w2t_bf', [16, 512, 128], BF16)
    pb.dt('wo_bf', [D, D], BF16)
    return [('wqn_t', 'wqn_bf', 'l0'), ('wqp_t', 'wqp_bf', 'l0'),
            ('wkva_t', 'wkva_bf', 'l0'), ('w1', 'w1_bf', 'l0'),
            ('w2t', 'w2t_bf', 'l0'), ('wo_t', 'wo_bf', 'l0')]


FFN_WEIGHTS = [('wg_t', [D, HID], F32), ('wu_t', [D, HID], F32),
               ('wd_t', [HID, D], F32)]


def build_upto(last='ffn0', out_kind='ExternalOutput', dbg=False):
    pb = PB()
    nc = pb.nc
    pb.register_consts(ALL_CONSTS)
    pb.dt('x_in', [NSEQ, D, S], F32, kind='ExternalInput')
    wlist = CONST_INPUTS + L0_WEIGHTS
    if last != 'l0':
        wlist = wlist + FFN_WEIGHTS
    for nm, shp, dt in wlist:
        pb.dt(nm, shp, dt, kind='ExternalInput')
    pb.dt('xhat_scr', [NSEQ, D, S], BF16)
    pb.dt('q_nope_scr', [NSEQ, D, S], BF16)
    pb.dt('q_pe_scr', [NSEQ, 1024, S], BF16)
    pb.dt('xr0', [NSEQ, D, S], F32,
          kind=out_kind if (last == 'l0' or dbg) else 'Internal')
    casts = l0_casts(pb)
    if dbg:
        pb.dt('dbg_xhat', [128, 16, 512], BF16, kind='ExternalOutput')
        pb.dt('dbg_gated', [128, 4, 512], BF16, kind='ExternalOutput')
    if last != 'l0':
        pb.dt('xr1', [NSEQ, D, S], F32,
              kind=out_kind if last == 'ffn0' else 'Internal',
              ) if not dbg else pb.dt('xr1', [NSEQ, D, S], F32, kind='ExternalOutput')
        pb.dt('wg_bf', [D, HID], BF16)
        pb.dt('wu_bf', [D, HID], BF16)
        pb.dt('wd_bf', [HID, D], BF16)
        casts += [('wg_t', 'wg_bf', 'ffn'), ('wu_t', 'wu_bf', 'ffn'),
              ('wd_t', 'wd_bf', 'ffn')]
    stage_init(pb, casts=casts)
    for s in range(NSEQ):
        stage_l0a1(pb, s)
        with ExitStack() as ckv:
            kvb = {
                'kv_tm': ckv.enter_context(nc.sbuf_tensor(f'kv_tm_{s}', [128, 16, 512], BF16)),
                'kv_fm': ckv.enter_context(nc.sbuf_tensor(f'kv_fm_{s}', [128, 4, 2048], BF16)),
                'kpe_pre': ckv.enter_context(nc.sbuf_tensor(f'kpe_pre_{s}', [64, 2048], BF16)),
                'kpe_fm': ckv.enter_context(nc.sbuf_tensor(f'kpe_fm_{s}', [64, 2048], BF16)),
            }
            stage_l0a2(pb, s, kvb)
            stage_l0b(pb, s, kvb)
    if last != 'l0':
        stage_ffn0(pb, dbg=dbg)
    pb.nc.finalize()
    return pb


def build_upto(last='ffn0', out_kind='ExternalOutput', dbg=False):
    pb = PB()
    nc = pb.nc
    pb.register_consts(ALL_CONSTS)
    pb.dt('x_in', [NSEQ, D, S], F32, kind='ExternalInput')
    wlist = CONST_INPUTS + L0_WEIGHTS
    if last != 'l0':
        wlist = wlist + FFN_WEIGHTS
    for nm, shp, dt in wlist:
        pb.dt(nm, shp, dt, kind='ExternalInput')
    pb.dt('xhat_scr', [NSEQ, D, S], BF16)
    pb.dt('q_nope_scr', [NSEQ, D, S], BF16)
    pb.dt('q_pe_scr', [NSEQ, 1024, S], BF16)
    pb.dt('xr0', [NSEQ, D, S], F32,
          kind=out_kind if (last == 'l0' or dbg) else 'Internal')
    casts = l0_casts(pb)
    if dbg:
        pb.dt('dbg_xhat', [128, 16, 512], BF16, kind='ExternalOutput')
        pb.dt('dbg_gated', [128, 4, 512], BF16, kind='ExternalOutput')
    if last != 'l0':
        pb.dt('xr1', [NSEQ, D, S], F32,
              kind=out_kind if last == 'ffn0' else 'Internal',
              ) if not dbg else pb.dt('xr1', [NSEQ, D, S], F32, kind='ExternalOutput')
        pb.dt('wg_bf', [D, HID], BF16)
        pb.dt('wu_bf', [D, HID], BF16)
        pb.dt('wd_bf', [HID, D], BF16)
        casts += [('wg_t', 'wg_bf', 'ffn'), ('wu_t', 'wu_bf', 'ffn'),
              ('wd_t', 'wd_bf', 'ffn')]
    stage_init(pb, casts=casts)
    for s in range(NSEQ):
        stage_l0a1(pb, s)
        with ExitStack() as ckv:
            kvb = {
                'kv_tm': ckv.enter_context(nc.sbuf_tensor(f'kv_tm_{s}', [128, 16, 512], BF16)),
                'kv_fm': ckv.enter_context(nc.sbuf_tensor(f'kv_fm_{s}', [128, 4, 2048], BF16)),
                'kpe_pre': ckv.enter_context(nc.sbuf_tensor(f'kpe_pre_{s}', [64, 2048], BF16)),
                'kpe_fm': ckv.enter_context(nc.sbuf_tensor(f'kpe_fm_{s}', [64, 2048], BF16)),
            }
            stage_l0a2(pb, s, kvb)
            stage_l0b(pb, s, kvb)
    if last != 'l0':
        stage_ffn0(pb, dbg=dbg)
    pb.nc.finalize()
    return pb


def build_l0(xr0_kind='ExternalOutput'):
    return build_upto(last='l0', out_kind=xr0_kind)


def input_names(pb):
    return [k for k, t in pb.dram.items()
            if getattr(t, 'kind', None) == 'ExternalInput' or
            (hasattr(t, 'tensor') and t.tensor.kind == 'ExternalInput')]



def stage_ffn0(pb, dbg=False):
    nc = pb.nc
    st = Stage(nc, 'ffn0')
    pb.fresh_engine_sems('ffn0')
    d = pb.dram
    mm, ev_v, ev_s = pb.sem('mm'), pb.sem('ev_v'), pb.sem('ev_s')
    ld = pb.sem('dX')
    ldS = [pb.sem('dS0'), pb.sem('dS1')]
    goQ = [pb.sem('dO0'), pb.sem('dO1')]
    with ExitStack() as c:
        wg = [c.enter_context(nc.sbuf_tensor(f'f_wg{b}', [128, 16, 512], BF16))
              for b in range(2)]
        wu = [c.enter_context(nc.sbuf_tensor(f'f_wu{b}', [128, 16, 512], BF16))
              for b in range(2)]
        wd = [c.enter_context(nc.sbuf_tensor(f'f_wd{b}', [128, 4, 2048], BF16))
              for b in range(2)]
        x1g = c.enter_context(nc.sbuf_tensor('f_x1g', [128, 16, 512], F32))
        xhat = c.enter_context(nc.sbuf_tensor('f_xhat', [128, 16, 512], BF16))
        y = x1g
        gbf = [c.enter_context(nc.sbuf_tensor(f'f_gbf{b}', [128, 512], BF16))
               for b in range(2)]
        ubf = [c.enter_context(nc.sbuf_tensor(f'f_ubf{b}', [128, 512], BF16))
               for b in range(2)]
        gated = c.enter_context(nc.sbuf_tensor('f_gated', [128, 4, 512], BF16))
        x2t = [c.enter_context(nc.sbuf_tensor(f'f_x2t{b}', [128, 512], F32))
               for b in range(2)]
        L = alloc_ln(pb, c, 'f')
        pG = [L['pB1'],
              c.enter_context(nc.psum_tensor('f_pG1', [128, 512], F32))]
        pU = [L['pB2'],
              c.enter_context(nc.psum_tensor('f_pU1', [128, 512], F32))]
        pD = [c.enter_context(nc.psum_tensor(f'f_pD{b}', [128, 512], F32))
              for b in range(2)]
        sil_ev = {0: (ev_s, 0), 1: (ev_s, 0)}
        ub_ev = {0: (ev_v, 0), 1: (ev_v, 0)}
        pD_ev = {0: (ev_v, 0), 1: (ev_v, 0)}
        wld = {}
        wused = {0: (mm, 0), 1: (mm, 0)}

        def load_slabs(hs):
            b = hs % 2
            w = [wused[b], pb.cast_grp['ffn']]
            sm = ldS[b]
            st.dma('sync', wg[b][:, :, :],
                   fm3(d['wg_bf'])[:, :, hs * 512:(hs + 1) * 512], sm, wait=w)
            st.dma('sync', wu[b][:, :, :],
                   fm3(d['wu_bf'])[:, :, hs * 512:(hs + 1) * 512], sm)
            l2 = st.dma('sync', wd[b][:, :, :],
                        fm3(d['wd_bf'])[:, hs * 4:(hs + 1) * 4, :], sm)
            wld[hs] = (sm, l2)

        xr0 = d['xr0']; xr1 = d['xr1']
        st_done = {0: (goQ[0], goQ[0].n), 1: (goQ[1], goQ[1].n)}
        for g in range(NG):
            s, gg = divmod(g, 4)
            t0 = gg * GSZ
            x1v = fm3(xr0[s])[:, :, t0:t0 + GSZ]
            x2v = fm3(xr1[s])[:, :, t0:t0 + GSZ]
            w = [(ev_v, ev_v.n), (ev_s, ev_s.n), (mm, mm.n)] if g else []
            xld = st.dma('sync', x1g[:, :, :], x1v, ld, wait=w)
            wxh = [(mm, mm.n)] if g else []
            emit_ln(pb, st, L, x1g, xhat, wait_x=[(ld, xld)], wait_xhat=wxh)
            xhat_ready = ev_v.n
            load_slabs(0)
            load_slabs(1)
            for hs in range(16):
                b = hs % 2
                if hs + 2 < 16:
                    pass
                for mc in range(4):
                    for k in range(16):
                        w = []
                        if k == 0:
                            w = [wld[hs], (ev_v, xhat_ready), sil_ev[mc % 2]]
                        st.op('tensor', 'matmul', pG[mc % 2][:, :],
                              wg[b][:, k, mc * 128:(mc + 1) * 128], xhat[:, k, :],
                              start=(k == 0), stop=(k == 15), wait=w,
                              inc=(mm, 1) if k == 15 else None)
                    for k in range(16):
                        w = ([ub_ev[mc % 2]] +
                             ([(ev_v, xhat_ready)] if hs == 0 else [])) \
                            if k == 0 else []
                        st.op('tensor', 'matmul', pU[mc % 2][:, :],
                              wu[b][:, k, mc * 128:(mc + 1) * 128], xhat[:, k, :],
                              start=(k == 0), stop=(k == 15), wait=w,
                              inc=(mm, 1) if k == 15 else None)
                    st.op('scalar', 'activation', gbf[mc % 2][:, :], pG[mc % 2][:, :],
                          AF.Silu, wait=[(mm, mm.n - 1), (ev_s, ev_s.n)], inc=(ev_s, 1))
                    sil_ev[mc % 2] = (ev_s, ev_s.n)
                    st.op('vector', 'tensor_copy', ubf[mc % 2][:, :], pU[mc % 2][:, :],
                          wait=[(mm, mm.n), (ev_v, ev_v.n)], inc=(ev_v, 1))
                    st.op('vector', 'tensor_mul', gated[:, mc, :], gbf[mc % 2][:, :],
                          ubf[mc % 2][:, :], wait=[(ev_s, sil_ev[mc % 2][1]),
                                                   (ev_v, ev_v.n), (mm, mm.n)],
                          inc=(ev_v, 1))
                    ub_ev[mc % 2] = (ev_v, ev_v.n)
                gated_ready = ev_v.n
                if dbg and g == 0 and hs == 0:
                    st.dma('scalar', pb.dram['dbg_xhat'][:, :, :], xhat[:, :, :],
                           goQ[0], wait=[(ev_v, xhat_ready)])
                    st.dma('scalar', pb.dram['dbg_gated'][:, :, :], gated[:, :, :],
                           goQ[0], wait=[(ev_v, gated_ready)])
                for dc in range(16):
                    db = dc % 2
                    for kk in range(4):
                        w = [(ev_v, gated_ready), pD_ev[db]] if kk == 0 else []
                        st.op('tensor', 'matmul', pD[db][:, :],
                              wd[b][:, kk, dc * 128:(dc + 1) * 128], gated[:, kk, :],
                              start=(kk == 0), stop=(kk == 3), wait=w,
                              inc=(mm, 1) if kk == 3 else None)
                    w_ev = [(mm, mm.n)]
                    if hs == 0:
                        w_ev.append((ev_v, xhat_ready))
                    st.op('vector', 'tensor_add', y[:, dc, :], y[:, dc, :],
                          pD[db][:, :], wait=w_ev, inc=(ev_v, 1))
                    pD_ev[db] = (ev_v, ev_v.n)
                wused[b] = (mm, mm.n)
                if hs + 2 < 16:
                    load_slabs(hs + 2)
            for cc in range(16):
                b2 = cc % 2
                st.op('vector', 'tensor_copy', x2t[b2][:, :], y[:, cc, :],
                      wait=[(ev_v, ev_v.n), st_done[b2]], inc=(ev_v, 1))
                st.dma('scalar', x2v[:, cc, :], x2t[b2][:, :], goQ[b2],
                       wait=[(ev_v, ev_v.n)])
                st_done[b2] = (goQ[b2], goQ[b2].n)
        for sm in (goQ[0], goQ[1]):
            st.wait('scalar', sm, sm.n)
        st.realize()



def stage_l1p(pb, which):
    nc = pb.nc
    st = Stage(nc, f'l1p_{which}')
    pb.fresh_engine_sems(f'l1p_{which}')
    d = pb.dram
    mm, ev_v, ev_s = pb.sem('mm'), pb.sem('ev_v'), pb.sem('ev_s')
    ld = pb.sem('dX')
    gw = pb.sem('dW')
    gox = pb.sem('dOx')
    goQ = [pb.sem('dO0'), pb.sem('dO1')]
    wsrc = {'q': 'l1wq_bf', 'k': 'l1wk_bf', 'v': 'l1wv_bf'}[which]
    with ExitStack() as c:
        W = c.enter_context(nc.sbuf_tensor(f'p_{which}_W', [128, 16, 2048], BF16))
        xhat = c.enter_context(nc.sbuf_tensor(f'p_{which}_xh', [128, 16, 512], BF16))
        ev = [c.enter_context(nc.sbuf_tensor(f'p_{which}_ev{b}', [128, 512], BF16))
              for b in range(2)]
        pq = [c.enter_context(nc.psum_tensor(f'p_{which}_pq{b}', [128, 512], F32))
              for b in range(2)]
        if which == 'q':
            xg = c.enter_context(nc.sbuf_tensor('p_q_xg', [128, 16, 512], F32))
            L = alloc_ln(pb, c, 'p_q')
            qg = c.enter_context(nc.sbuf_tensor('p_q_qg', [128, 16, 512], BF16))
            ev2 = [c.enter_context(nc.sbuf_tensor(f'p_q_ev2{b}', [128, 2048], BF16))
                   for b in range(2)]
        if which == 'v':
            bvbc = c.enter_context(nc.sbuf_tensor('p_v_bvbc', [128, 2048], F32))
            pbc = c.enter_context(nc.psum_tensor('p_v_pbc', [128, 512], F32))
            bvr = c.enter_context(nc.sbuf_tensor('p_v_bvr', [1, 2048], F32))
            st.dma('sync', bvr[:, :], d['bv_row'][:, :], gw)
            for cc in range(4):
                st.op('tensor', 'matmul', pbc[:, :], pb.sb['onesr_f'][:, :],
                      bvr[:, cc * 512:(cc + 1) * 512], start=True, stop=True,
                      wait=[(gw, gw.n), (ev_v, ev_v.n)], inc=(mm, 1))
                st.op('vector', 'tensor_copy', bvbc[:, cc * 512:(cc + 1) * 512],
                      pbc[:, :], wait=[(mm, mm.n)], inc=(ev_v, 1))
        for k in range(16):
            st.dma('sync', W[:, k, :], fm3(d[wsrc])[:, k, :], gw,
                   wait=[pb.cast_grp['l1a']] if k == 0 else ())
        w_ready = gw.n
        evd = {0: (ev_s, 0), 1: (ev_s, 0)}
        std = {0: (goQ[0], goQ[0].n), 1: (goQ[1], goQ[1].n)}
        vevd = {0: (ev_v, 0), 1: (ev_v, 0)}
        for g in range(NG):
            s, gg = divmod(g, 4)
            t0 = gg * GSZ
            if which == 'q':
                w = [(ev_v, ev_v.n), (ev_s, ev_s.n), (mm, mm.n)] if g else []
                xld = st.dma('sync', xg[:, :, :], fm3(d['xr1'][s])[:, :, t0:t0 + GSZ],
                             ld, wait=w)
                wxh = [(mm, mm.n), (gox, gox.n)] if g else []
                emit_ln(pb, st, L, xg, xhat, wait_x=[(ld, xld)], wait_xhat=wxh)
                xr_ = ev_v.n
                st.dma('scalar', fm3(d['xhat_scr'][s])[:, :, t0:t0 + GSZ],
                       xhat[:, :, :], gox, wait=[(ev_v, xr_)])
                xhat_ready = [(ev_v, xr_)]
            else:
                w = [(mm, mm.n)] if g else []
                xld = st.dma('sync', xhat[:, :, :],
                             fm3(d['xhat_scr'][s])[:, :, t0:t0 + GSZ], ld, wait=w)
                xhat_ready = [(ld, xld)]
            if which == 'q':
                bias_c = pb.sb['bq_c_sb']
                shuf_done = ev_v.n
                for m in range(16):
                    b = m % 2
                    for k in range(16):
                        w = list(xhat_ready) + [evd[b]] if k == 0 else []
                        if k == 0 and m == 0 and g == 0:
                            w.append((gw, w_ready))
                        st.op('tensor', 'matmul', pq[b][:, :],
                              W[:, k, m * 128:(m + 1) * 128], xhat[:, k, :],
                              start=(k == 0), stop=(k == 15), wait=w,
                              inc=(mm, 1) if k == 15 else None)
                    st.op('vector', 'tensor_scalar_add', qg[:, m, :], pq[b][:, :],
                          bias_c[:, m:m + 1],
                          wait=[(mm, mm.n), (ev_v, ev_v.n)], inc=(ev_v, 1))
                    evd[b] = (ev_v, ev_v.n)
                q_ready = ev_v.n
                for tc in range(4):
                    h = gg * 4 + tc
                    b2 = tc % 2
                    src_ap = qg[:, :, tc * 128:(tc + 1) * 128].transpose([0, 2, 1])
                    st.op('vector', 'tensor_copy', ev2[b2][:, :], src_ap,
                          wait=[(ev_v, q_ready), std[b2]], inc=(ev_v, 1))
                    st.dma('scalar', fm3(d['q1_scr'][s])[:, h, :], ev2[b2][:, :],
                           goQ[b2], wait=[(ev_v, ev_v.n)])
                    std[b2] = (goQ[b2], goQ[b2].n)
                shuf_done = ev_v.n
            elif which == 'k':
                bias_c = pb.sb['bk_c_sb']
                dst = fm3(d['k_scr'][s])
                for m in range(16):
                    b = m % 2
                    for k in range(16):
                        w = list(xhat_ready) + [evd[b]] if k == 0 else []
                        if k == 0 and m == 0 and g == 0:
                            w.append((gw, w_ready))
                        st.op('tensor', 'matmul', pq[b][:, :],
                              W[:, k, m * 128:(m + 1) * 128], xhat[:, k, :],
                              start=(k == 0), stop=(k == 15), wait=w,
                              inc=(mm, 1) if k == 15 else None)
                    st.op('vector', 'tensor_scalar_add', ev[b][:, :], pq[b][:, :],
                          bias_c[:, m:m + 1],
                          wait=[(mm, mm.n), std[b], (ev_v, ev_v.n)], inc=(ev_v, 1))
                    evd[b] = (ev_v, ev_v.n)
                    st.dma('scalar', dst[:, m, t0:t0 + GSZ], ev[b][:, :], goQ[b],
                           wait=[(ev_v, ev_v.n)])
                    std[b] = (goQ[b], goQ[b].n)
            else:
                for tc in range(4):
                    for ns in range(4):
                        b = (tc * 4 + ns) % 2
                        for k in range(16):
                            w = list(xhat_ready) + [vevd[b]] if k == 0 else []
                            if k == 0 and tc == 0 and ns == 0 and g == 0:
                                w.append((gw, w_ready))
                            st.op('tensor', 'matmul', pq[b][:, :],
                                  xhat[:, k, tc * 128:(tc + 1) * 128],
                                  W[:, k, ns * 512:(ns + 1) * 512],
                                  start=(k == 0), stop=(k == 15), wait=w,
                                  inc=(mm, 1) if k == 15 else None)
                        st.op('vector', 'tensor_add', ev[b][:, :], pq[b][:, :],
                              bvbc[:, ns * 512:(ns + 1) * 512],
                              wait=[(mm, mm.n), std[b], (ev_v, ev_v.n)], inc=(ev_v, 1))
                        vevd[b] = (ev_v, ev_v.n)
                        st.dma('scalar',
                               d['v_scr'][s, t0 + tc * 128:t0 + (tc + 1) * 128,
                                          ns * 512:(ns + 1) * 512],
                               ev[b][:, :], goQ[b], wait=[(ev_v, ev_v.n)])
                        std[b] = (goQ[b], goQ[b].n)
        for sm in (gox, goQ[0], goQ[1]):
            st.wait('scalar', sm, sm.n)
        st.realize()



def stage_l1b(pb, s, half):
    nc = pb.nc
    st = Stage(nc, f'l1b_s{s}_h{half}')
    pb.fresh_engine_sems(f'l1b_s{s}_h{half}')
    d = pb.dram
    mm, ev_v, ev_s = pb.sem('mm'), pb.sem('ev_v'), pb.sem('ev_s')
    gw = pb.sem('dW')
    ldQ = [pb.sem('dQ0'), pb.sem('dQ1')]
    gox = pb.sem('dOx')
    iden = pb.sb['iden_bf']; maskd = pb.sb['maskd_f']
    with ExitStack() as c:
        kf = c.enter_context(nc.sbuf_tensor(f'lb_kf_{s}{half}', [128, 8, 2048], BF16))
        vf = c.enter_context(nc.sbuf_tensor(f'lb_vf_{s}{half}', [128, 16, 1024], BF16))
        qt_b = [c.enter_context(nc.sbuf_tensor(f'lb_qt{b}_{s}{half}', [128, 8, 128], BF16))
                for b in range(2)]
        ssb = [c.enter_context(nc.sbuf_tensor(f'lb_ssb{p}_{s}{half}', [128, 2048], F32))
               for p in range(3)]
        ebf = [c.enter_context(nc.sbuf_tensor(f'lb_ebf{p}_{s}{half}', [128, 2048], BF16))
               for p in range(3)]
        etb = [c.enter_context(nc.sbuf_tensor(f'lb_etb{p}_{s}{half}', [128, 16, 128], BF16))
               for p in range(2)]
        ao = c.enter_context(nc.sbuf_tensor(f'lb_ao_{s}{half}', [128, 8, 512], BF16))
        rowsb = [c.enter_context(nc.sbuf_tensor(f'lb_rows{p}_{s}{half}', [128, 4], F32))
                 for p in range(3)]
        psc = [c.enter_context(nc.psum_tensor(f'lb_psc{b}_{s}{half}', [128, 512], F32))
               for b in range(4)]
        pet = [c.enter_context(nc.psum_tensor(f'lb_pet{b}_{s}{half}', [128, 128], BF16))
               for b in range(2)]
        pao = [c.enter_context(nc.psum_tensor(f'lb_pao{p}_{s}{half}', [128, 128], F32))
               for p in range(2)]
        for k in range(8):
            st.dma('sync', kf[:, k, :], fm3(d['k_scr'][s])[:, half * 8 + k, :], gw)
        vv = d['v_scr'][s].rearrange('(tc p) n -> p tc n', p=128)
        for tc in range(4):
            st.dma('sync', vf[:, tc * 4:(tc + 1) * 4, :],
                   vv[:, tc * 4:(tc + 1) * 4, half * 1024:(half + 1) * 1024], gw)
        w_ready = gw.n
        qs = d['q1_scr'][s].rearrange('(c p) t -> p c t', p=128)
        q_used = {0: (mm, 0), 1: (mm, 0)}
        sc_ev = {i: (ev_s, 0) for i in range(4)}
        et_ev = {0: (ev_v, 0), 1: (ev_v, 0)}
        ssb_used = {i: (ev_s, 0) for i in range(3)}
        e_used = {i: (mm, 0) for i in range(3)}
        etb_used = {0: (mm, 0), 1: (mm, 0)}
        pao_ev = {0: (ev_s, 0), 1: (ev_s, 0)}
        ao_st = (gox, gox.n)
        kbi = 0
        for qt in range(16):
            b = qt % 2
            w = [q_used[b]]
            if qt == 0:
                w.append((gw, w_ready))
            l1 = st.dma('sync', qt_b[b][:, :, :],
                        qs[:, half * 8:half * 8 + 8, qt * 128:(qt + 1) * 128],
                        ldQ[b], wait=w)
            v = qt + 1
            kgs = (v + 3) // 4
            state = {}
            sdone = {}

            def p1_pe(h):
                nonlocal kbi
                kglist = []
                for kg in range(kgs):
                    kb = kbi % 4
                    kbi += 1
                    wdt = min(4, v - kg * 4) * 128
                    w = [(ldQ[b], l1), sc_ev[kb]]
                    st.op('tensor', 'matmul', psc[kb][:, 0:wdt], qt_b[b][:, h, :],
                          kf[:, h, kg * 512:kg * 512 + wdt], start=True, stop=True,
                          wait=w, inc=(mm, 1))
                    kglist.append((kg, kb, wdt, mm.n))
                sdone[h] = kglist

            def p1_rest(h):
                p = h % 3
                negmax = rowsb[p][:, 0:1]; negmaxs = rowsb[p][:, 1:2]
                rs = rowsb[p][:, 2:3]
                for kg, kb, wdt, mmv in sdone.pop(h):
                    if kg == kgs - 1:
                        dcol = wdt - 128
                        st.op('vector', 'tensor_add', psc[kb][:, dcol:dcol + 128],
                              psc[kb][:, dcol:dcol + 128], maskd[:, :],
                              wait=[(mm, mmv), (ev_v, ev_v.n)], inc=(ev_v, 1))
                        st.op('scalar', 'copy', ssb[p][:, kg * 512:kg * 512 + wdt],
                              psc[kb][:, 0:wdt],
                              wait=[(ev_v, ev_v.n), ssb_used[p], (ev_s, ev_s.n)],
                              inc=(ev_s, 1))
                    else:
                        st.op('scalar', 'copy', ssb[p][:, kg * 512:kg * 512 + wdt],
                              psc[kb][:, 0:wdt],
                              wait=[(mm, mmv), ssb_used[p], (ev_s, ev_s.n)],
                              inc=(ev_s, 1))
                    sc_ev[kb] = (ev_s, ev_s.n)
                st.op('vector', 'reduce_max', negmax, ssb[p][:, 0:v * 128], axis=AX.X,
                      negate=True, wait=[(ev_s, ev_s.n), (ev_v, ev_v.n)], inc=(ev_v, 1))
                st.op('vector', 'tensor_scalar_mul', negmaxs, negmax, float(L1_SCALE),
                      wait=[(ev_v, ev_v.n)], inc=(ev_v, 1))
                st.op('scalar', 'activation', ebf[p][:, 0:v * 128], ssb[p][:, 0:v * 128],
                      AF.Exp, bias=negmaxs, scale=float(L1_SCALE), accum_out=rs,
                      wait=[(ev_v, ev_v.n), e_used[p], (ev_s, ev_s.n)], inc=(ev_s, 1))
                ssb_used[p] = (ev_s, ev_s.n)
                st.op('vector', 'reciprocal', rowsb[p][:, 3:4], rs,
                      wait=[(ev_s, ev_s.n), (ev_v, ev_v.n)], inc=(ev_v, 1))
                st.op('vector', 'tensor_scalar_mul', ebf[p][:, 0:v * 128],
                      ebf[p][:, 0:v * 128], rowsb[p][:, 3:4],
                      wait=[(ev_v, ev_v.n)], inc=(ev_v, 1))
                state[h] = (ev_v, ev_v.n)

            def phase2(h):
                p = h % 3
                p2 = h % 2
                e_ready = state.pop(h)
                for kt in range(v):
                    tb = kt % 2
                    st.op('tensor', 'transpose', pet[tb][:, :],
                          ebf[p][:, kt * 128:(kt + 1) * 128], iden[:, :],
                          wait=[e_ready, et_ev[tb]], inc=(mm, 1))
                    st.op('vector', 'tensor_copy', etb[p2][:, kt, :], pet[tb][:, :],
                          wait=[(mm, mm.n), etb_used[p2] if kt == 0 else (ev_v, 0)],
                          inc=(ev_v, 1))
                    et_ev[tb] = (ev_v, ev_v.n)
                e_used[p] = (mm, mm.n)
                for kt in range(v):
                    w = [(ev_v, ev_v.n), pao_ev[p2]] if kt == 0 else []
                    st.op('tensor', 'matmul', pao[p2][:, :],
                          vf[:, kt, h * 128:(h + 1) * 128], etb[p2][:, kt, :],
                          start=(kt == 0), stop=(kt == v - 1), wait=w,
                          inc=(mm, 1) if kt == v - 1 else None)
                etb_used[p2] = (mm, mm.n)
                st.op('scalar', 'copy', ao[:, h, (qt % 4) * 128:(qt % 4 + 1) * 128],
                      pao[p2][:, :], wait=[(mm, mm.n), ao_st, (ev_s, ev_s.n)],
                      inc=(ev_s, 1))
                pao_ev[p2] = (ev_s, ev_s.n)

            for h in range(10):
                if h < 8:
                    p1_pe(h)
                if h >= 2:
                    phase2(h - 2)
                if h < 8:
                    p1_rest(h)
            q_used[b] = (mm, mm.n)
            if qt % 4 == 3:
                t0 = (qt - 3) * 128
                st.dma('scalar',
                       fm3(d['ao_scr'][s])[:, half * 8:half * 8 + 8, t0:t0 + GSZ],
                       ao[:, :, :], gox, wait=[(ev_s, ev_s.n), ao_st])
                ao_st = (gox, gox.n)
        st.wait('scalar', gox, gox.n)
        st.realize()



def stage_l1wo(pb):
    nc = pb.nc
    st = Stage(nc, 'l1wo')
    pb.fresh_engine_sems('l1wo')
    d = pb.dram
    mm, ev_v, ev_s = pb.sem('mm'), pb.sem('ev_v'), pb.sem('ev_s')
    ld = pb.sem('dX')
    gw = pb.sem('dW')
    ldR = [pb.sem('dR0'), pb.sem('dR1')]
    goQ = [pb.sem('dO0'), pb.sem('dO1')]
    with ExitStack() as c:
        W = c.enter_context(nc.sbuf_tensor('wo1_W', [128, 16, 2048], BF16))
        aog = c.enter_context(nc.sbuf_tensor('wo1_ao', [128, 16, 512], BF16))
        x2c = [c.enter_context(nc.sbuf_tensor(f'wo1_x2{b}', [128, 512], F32))
               for b in range(2)]
        x3t = [c.enter_context(nc.sbuf_tensor(f'wo1_x3{b}', [128, 512], F32))
               for b in range(2)]
        pq = [c.enter_context(nc.psum_tensor(f'wo1_pq{b}', [128, 512], F32))
              for b in range(2)]
        for k in range(16):
            st.dma('sync', W[:, k, :], fm3(d['l1wo_bf'])[:, k, :], gw,
                   wait=[pb.cast_grp['l1a']] if k == 0 else ())
        w_ready = gw.n
        bo = pb.sb['bo_c_sb']
        evd = {0: (ev_v, 0), 1: (ev_v, 0)}
        std = {0: (goQ[0], goQ[0].n), 1: (goQ[1], goQ[1].n)}
        xu = {0: (ev_v, 0), 1: (ev_v, 0)}
        for g in range(NG):
            s, gg = divmod(g, 4)
            t0 = gg * GSZ
            w = [(mm, mm.n)] if g else [(gw, w_ready)]
            ald = st.dma('sync', aog[:, :, :], fm3(d['ao_scr'][s])[:, :, t0:t0 + GSZ],
                         ld, wait=w)
            for m in range(16):
                b = m % 2
                xl = st.dma('sync', x2c[b][:, :], fm3(d['xr1'][s])[:, m, t0:t0 + GSZ],
                            ldR[b], wait=[xu[b]])
                for k in range(16):
                    w = [(ld, ald), evd[b]] if k == 0 else []
                    st.op('tensor', 'matmul', pq[b][:, :],
                          W[:, k, m * 128:(m + 1) * 128], aog[:, k, :],
                          start=(k == 0), stop=(k == 15), wait=w,
                          inc=(mm, 1) if k == 15 else None)
                st.op('vector', 'scalar_tensor_tensor', x3t[b][:, :], pq[b][:, :],
                      bo[:, m:m + 1], x2c[b][:, :], ALU.add, ALU.add,
                      wait=[(mm, mm.n), (ldR[b], xl), std[b], (ev_v, ev_v.n)],
                      inc=(ev_v, 1))
                evd[b] = (ev_v, ev_v.n)
                xu[b] = (ev_v, ev_v.n)
                st.dma('scalar', fm3(d['xr0'][s])[:, m, t0:t0 + GSZ], x3t[b][:, :],
                       goQ[b], wait=[(ev_v, ev_v.n)])
                std[b] = (goQ[b], goQ[b].n)
        for sm in (goQ[0], goQ[1]):
            st.wait('scalar', sm, sm.n)
        st.realize()



def stage_wcomb(pb):
    nc = pb.nc
    st = Stage(nc, 'wcomb')
    pb.fresh_engine_sems('wcomb')
    d = pb.dram
    mm, ev_v, ev_s = pb.sem('mm'), pb.sem('ev_v'), pb.sem('ev_s')
    gw = pb.sem('dW')
    ldS = [pb.sem('dS0'), pb.sem('dS1')]
    goQ = [pb.sem('dO0'), pb.sem('dO1')]
    TWO_PI = float(2 * np.pi)
    with ExitStack() as c:
        phr = c.enter_context(nc.sbuf_tensor('wc_phr', [1, 8192], F32))
        modr = c.enter_context(nc.sbuf_tensor('wc_modr', [1, 8192], F32))
        rtmp = c.enter_context(nc.sbuf_tensor('wc_rtmp', [1, 8192], F32))
        rtmp2 = c.enter_context(nc.sbuf_tensor('wc_rtmp2', [1, 8192], F32))
        pw0 = [c.enter_context(nc.sbuf_tensor(f'wc_pw0{b}', [128, 512], F32))
               for b in range(2)]
        pw1 = [c.enter_context(nc.sbuf_tensor(f'wc_pw1{b}', [128, 512], F32))
               for b in range(2)]
        wcb = [c.enter_context(nc.sbuf_tensor(f'wc_wcb{b}', [128, 512], BF16))
               for b in range(2)]
        t1 = c.enter_context(nc.sbuf_tensor('wc_t1', [128, 512], F32))
        mpk = c.enter_context(nc.sbuf_tensor('wc_mpk', [128, 64], F32))
        ptmp = c.enter_context(nc.sbuf_tensor('wc_ptmp', [128, 64], F32))
        pb0 = c.enter_context(nc.psum_tensor('wc_pb0', [128, 512], F32))
        pb1 = c.enter_context(nc.psum_tensor('wc_pb1', [128, 512], F32))

        def range_wrap(dst, src_ap, tmp_ap, n):
            st.op('vector', 'tensor_single_scalar', dst, src_ap, float(np.pi),
                  ALU.is_gt, wait=[(ev_v, ev_v.n), (gw, gw.n), (ev_s, ev_s.n)],
                  inc=(ev_v, 1))
            st.op('vector', 'tensor_single_scalar', tmp_ap, src_ap, float(-np.pi),
                  ALU.is_lt, wait=[(ev_v, ev_v.n)], inc=(ev_v, 1))
            st.op('vector', 'tensor_sub', tmp_ap, tmp_ap, dst,
                  wait=[(ev_v, ev_v.n)], inc=(ev_v, 1))
            st.op('vector', 'tensor_scalar_mul', tmp_ap, tmp_ap, TWO_PI,
                  wait=[(ev_v, ev_v.n)], inc=(ev_v, 1))
            st.op('vector', 'tensor_add', dst, src_ap, tmp_ap,
                  wait=[(ev_v, ev_v.n)], inc=(ev_v, 1))

        def sincos_sum(dst, x_ap, a_ap, b_ap, n):
            st.op('scalar', 'activation', a_ap, x_ap, AF.Sin, bias=0.0, scale=1.0,
                  wait=[(ev_v, ev_v.n), (ev_s, ev_s.n)], inc=(ev_s, 1))
            st.op('scalar', 'activation', b_ap, x_ap, AF.Abs, bias=0.0, scale=1.0,
                  wait=[(ev_s, ev_s.n)], inc=(ev_s, 1))
            st.op('scalar', 'activation', b_ap, b_ap, AF.Sin,
                  bias=float(np.pi / 2), scale=-1.0, wait=[(ev_s, ev_s.n)],
                  inc=(ev_s, 1))
            st.op('vector', 'tensor_add', dst, a_ap, b_ap,
                  wait=[(ev_s, ev_s.n), (ev_v, ev_v.n)], inc=(ev_v, 1))

        for i in range(4):
            st.dma('sync', phr[:, i * 2048:(i + 1) * 2048], d['phase_r'][i:i + 1, :], gw)
        range_wrap(rtmp[:, :], phr[:, :], rtmp2[:, :], 8192)
        sincos_sum(modr[:, :], rtmp[:, :], rtmp2[:, :], phr[:, :], 8192)
        ph_pk = pb.sb['phase_c_sb']; pa_pk = pb.sb['pathb_c_sb']
        bc_pk = pb.sb['bcomb_c_sb']
        range_wrap(mpk[:, :], ph_pk[:, :], ptmp[:, :], 64)
        sincos_sum(mpk[:, :], mpk[:, :], ptmp[:, :], t1[:, 0:64], 64)
        for l in range(2):
            st.op('vector', 'tensor_mul', ptmp[:, 0:16], pa_pk[:, l * 32:l * 32 + 16],
                  mpk[:, l * 32:l * 32 + 16], wait=[(ev_v, ev_v.n)], inc=(ev_v, 1))
            st.op('vector', 'tensor_mul', ptmp[:, 16:32],
                  pa_pk[:, l * 32 + 16:l * 32 + 32],
                  mpk[:, l * 32 + 16:l * 32 + 32], wait=[(ev_v, ev_v.n)], inc=(ev_v, 1))
            st.op('vector', 'tensor_add', bc_pk[:, l * 16:(l + 1) * 16],
                  ptmp[:, 0:16], ptmp[:, 16:32], wait=[(ev_v, ev_v.n)], inc=(ev_v, 1))
        ldd = {0: (ev_v, 0), 1: (ev_v, 0)}
        std = {0: (goQ[0], goQ[0].n), 1: (goQ[1], goQ[1].n)}
        for l in range(2):
            for ccc in range(4):
                cs = ccc * 512
                st.op('tensor', 'matmul', pb0[:, :], pb.sb['onesr_f'][:, :],
                      modr[:, l * 4096 + cs:l * 4096 + cs + 512], start=True,
                      stop=True, wait=[(ev_v, ev_v.n), (mm, mm.n)], inc=(mm, 1))
                st.op('tensor', 'matmul', pb1[:, :], pb.sb['onesr_f'][:, :],
                      modr[:, l * 4096 + 2048 + cs:l * 4096 + 2048 + cs + 512],
                      start=True, stop=True, inc=(mm, 1))
                bc_done = mm.n
                for k in range(16):
                    b = k % 2
                    st.dma('sync', pw0[b][:, :],
                           d['pathw'][l, 0, k * 128:(k + 1) * 128, cs:cs + 512],
                           ldS[b], wait=[ldd[b]])
                    w2l = st.dma('sync', pw1[b][:, :],
                                 d['pathw'][l, 1, k * 128:(k + 1) * 128, cs:cs + 512],
                                 ldS[b])
                    st.op('vector', 'tensor_mul', t1[:, :], pw0[b][:, :], pb0[:, :],
                          wait=[(ldS[b], w2l), (mm, bc_done), (ev_v, ev_v.n)],
                          inc=(ev_v, 1))
                    st.op('vector', 'tensor_mul', wcb[b][:, :], pw1[b][:, :],
                          pb1[:, :], wait=[(ldS[b], w2l), (ev_v, ev_v.n), std[b]],
                          inc=(ev_v, 1))
                    st.op('vector', 'tensor_add', wcb[b][:, :], wcb[b][:, :],
                          t1[:, :], wait=[(ev_v, ev_v.n)], inc=(ev_v, 1))
                    ldd[b] = (ev_v, ev_v.n)
                    st.dma('scalar', d['wcomb_bf'][l, k * 128:(k + 1) * 128,
                                                   cs:cs + 512],
                           wcb[b][:, :], goQ[b], wait=[(ev_v, ev_v.n)])
                    std[b] = (goQ[b], goQ[b].n)
        for sm in (goQ[0], goQ[1]):
            st.wait('scalar', sm, sm.n)
        st.realize()



def stage_l1fa(pb):
    nc = pb.nc
    st = Stage(nc, 'l1fa')
    pb.fresh_engine_sems('l1fa')
    d = pb.dram
    mm, ev_v, ev_s = pb.sem('mm'), pb.sem('ev_v'), pb.sem('ev_s')
    ld = pb.sem('dX')
    ldS = [pb.sem('dS0'), pb.sem('dS1')]
    goQ = [pb.sem('dO0'), pb.sem('dO1')]
    with ExitStack() as c:
        x3g = c.enter_context(nc.sbuf_tensor('fa_x3g', [128, 16, 512], F32))
        xhat = c.enter_context(nc.sbuf_tensor('fa_xhat', [128, 16, 512], BF16))
        ampf = c.enter_context(nc.sbuf_tensor('fa_ampf', [128, 16, 512], F32))
        tf = c.enter_context(nc.sbuf_tensor('fa_tf', [128, 16, 512], F32))
        at = c.enter_context(nc.sbuf_tensor('fa_at', [128, 512], F32))
        s1 = c.enter_context(nc.sbuf_tensor('fa_s1', [128, 512], F32))
        s2 = c.enter_context(nc.sbuf_tensor('fa_s2', [128, 512], F32))
        hbf = [c.enter_context(nc.sbuf_tensor(f'fa_hbf{b}', [128, 512], BF16))
               for b in range(2)]
        wsl = [c.enter_context(nc.sbuf_tensor(f'fa_wsl{b}', [128, 16, 512], BF16))
               for b in range(2)]
        L = alloc_ln(pb, c, 'fa')
        pq = [c.enter_context(nc.psum_tensor(f'fa_pq{b}', [128, 512], F32))
              for b in range(2)]
        ampb = pb.sb['ampb_c_sb']; phb = pb.sb['phb_c_sb']
        wu = {0: (mm, 0), 1: (mm, 0)}
        std = {0: (goQ[0], goQ[0].n), 1: (goQ[1], goQ[1].n)}
        for g in range(NG):
            s, gg = divmod(g, 4)
            t0 = gg * GSZ
            w = [(ev_v, ev_v.n), (ev_s, ev_s.n), (mm, mm.n)] if g else []
            xld = st.dma('sync', x3g[:, :, :], fm3(d['xr0'][s])[:, :, t0:t0 + GSZ],
                         ld, wait=w)
            wxh = [(mm, mm.n)] if g else []
            emit_ln(pb, st, L, x3g, xhat, wait_x=[(ld, xld)], wait_xhat=wxh)
            xhat_ready = [(ev_v, ev_v.n)]
            for wi, (wnm, dstb, bias_c, fn) in enumerate(
                    (('ampw_bf', ampf, ampb, AF.Sigmoid),
                     ('phw_bf', tf, phb, AF.Tanh))):
                for sl in range(4):
                    b = sl % 2
                    w = [wu[b]]
                    if g == 0 and wi == 0 and sl < 2:
                        w.append(pb.cast_grp['l1f'])
                    wld = st.dma('sync', wsl[b][:, :, :],
                                 fm3(d[wnm])[:, :, sl * 512:(sl + 1) * 512], ldS[b],
                                 wait=w)
                    for mc in range(4):
                        ch = sl * 4 + mc
                        pbk = pq[mc % 2]
                        for k in range(16):
                            w = list(xhat_ready) + [(ldS[b], wld), (ev_s, ev_s.n)] \
                                if k == 0 else []
                            st.op('tensor', 'matmul', pbk[:, :],
                                  wsl[b][:, k, mc * 128:(mc + 1) * 128],
                                  xhat[:, k, :], start=(k == 0), stop=(k == 15),
                                  wait=w, inc=(mm, 1) if k == 15 else None)
                        st.op('scalar', 'activation', dstb[:, ch, :], pbk[:, :],
                              fn, bias=bias_c[:, ch:ch + 1], scale=1.0,
                              wait=[(mm, mm.n), (ev_v, ev_v.n)], inc=(ev_s, 1))
                    wu[b] = (mm, mm.n)
            for ch in range(16):
                st.op('scalar', 'activation', at[:, :], tf[:, ch, :], AF.Abs,
                      bias=0.0, scale=1.0, wait=[(ev_s, ev_s.n), (ev_v, ev_v.n)],
                      inc=(ev_s, 1))
                st.op('scalar', 'activation', s1[:, :], tf[:, ch, :], AF.Sin,
                      bias=0.0, scale=PI_LO, wait=[(ev_s, ev_s.n)], inc=(ev_s, 1))
                st.op('scalar', 'activation', s2[:, :], at[:, :], AF.Sin,
                      bias=float(np.pi / 2), scale=-PI_LO, wait=[(ev_s, ev_s.n)],
                      inc=(ev_s, 1))
                st.op('vector', 'tensor_add', s1[:, :], s1[:, :], s2[:, :],
                      wait=[(ev_s, ev_s.n), (ev_v, ev_v.n)], inc=(ev_v, 1))
                b = ch % 2
                st.op('vector', 'tensor_mul', hbf[b][:, :], s1[:, :],
                      ampf[:, ch, :], wait=[(ev_v, ev_v.n), std[b]], inc=(ev_v, 1))
                st.dma('scalar', fm3(d['h_scr'][s])[:, ch, t0:t0 + GSZ],
                       hbf[b][:, :], goQ[b], wait=[(ev_v, ev_v.n)])
                std[b] = (goQ[b], goQ[b].n)
        for sm in (goQ[0], goQ[1]):
            st.wait('scalar', sm, sm.n)
        st.realize()



def stage_l1fb(pb):
    nc = pb.nc
    st = Stage(nc, 'l1fb')
    pb.fresh_engine_sems('l1fb')
    d = pb.dram
    mm, ev_v, ev_s = pb.sem('mm'), pb.sem('ev_v'), pb.sem('ev_s')
    ld = pb.sem('dX')
    gh = pb.sem('dH')
    ldS = [pb.sem('dS0'), pb.sem('dS1')]
    ldR = [pb.sem('dR0'), pb.sem('dR1')]
    goQ = [pb.sem('dO0'), pb.sem('dO1')]
    with ExitStack() as c:
        hf = c.enter_context(nc.sbuf_tensor('fb_hf', [128, 16, 512], F32))
        hbf = c.enter_context(nc.sbuf_tensor('fb_hbf', [128, 16, 512], BF16))
        y = c.enter_context(nc.sbuf_tensor('fb_y', [128, 16, 512], F32))
        ebf = c.enter_context(nc.sbuf_tensor('fb_ebf', [128, 4, 512], BF16))
        ws = [c.enter_context(nc.sbuf_tensor(f'fb_ws{b}', [128, 16, 512], BF16))
              for b in range(2)]
        ps = [c.enter_context(nc.sbuf_tensor(f'fb_ps{b}', [128, 4, 2048], BF16))
              for b in range(2)]
        x3c = [c.enter_context(nc.sbuf_tensor(f'fb_x3c{b}', [128, 512], F32))
               for b in range(2)]
        x4t = [c.enter_context(nc.sbuf_tensor(f'fb_x4t{b}', [128, 512], F32))
               for b in range(2)]
        L = alloc_ln(pb, c, 'fb')
        pE = [c.enter_context(nc.psum_tensor(f'fb_pE{b}', [128, 512], F32))
              for b in range(2)]
        pP = [c.enter_context(nc.psum_tensor(f'fb_pP{b}', [128, 512], F32))
              for b in range(2)] + [L['pB1'], L['pB2']]
        entb = pb.sb['entb_c_sb']; entpb = pb.sb['entpb_c_sb']
        bcomb = pb.sb['bcomb_c_sb']
        ln3g, ln3b = pb.sb['ln3g_c_sb'], pb.sb['ln3b_c_sb']
        ln4g, ln4b = pb.sb['ln4g_c_sb'], pb.sb['ln4b_c_sb']
        wu = {0: (mm, 0), 1: (mm, 0)}
        stx = {0: (goQ[0], goQ[0].n), 1: (goQ[1], goQ[1].n)}
        pP_ev = {i: (ev_v, 0) for i in range(4)}
        pE_ev = {0: (ev_v, 0), 1: (ev_v, 0)}
        for g in range(NG):
            s, gg = divmod(g, 4)
            t0 = gg * GSZ
            w = [(ev_v, ev_v.n), (ev_s, ev_s.n), (mm, mm.n)] if g else []
            hld = st.dma('gpsimd', hf[:, :, :], fm3(d['h_scr'][s])[:, :, t0:t0 + GSZ],
                         gh, wait=w)
            hbl = st.dma('sync', hbf[:, :, :], fm3(d['h_scr'][s])[:, :, t0:t0 + GSZ],
                         ld)
            h_ready = [(gh, hld), (ld, hbl)]
            cur_h = list(h_ready)
            for l in range(2):
                wnm = f'entw{l}_bf'; pnm = f'entpw{l}_bf'; wcn = 'wcomb_bf'
                for hs in range(16):
                    b = hs % 2
                    w = [wu[b]]
                    if g == 0 and l == 0 and hs < 2:
                        w.append(pb.cast_grp['l1f'])
                    st.dma('sync', ws[b][:, :, :],
                           fm3(d[wnm])[:, :, hs * 512:(hs + 1) * 512], ldS[b],
                           wait=w)
                    pld = st.dma('sync', ps[b][:, :, :],
                                 fm3(d[pnm])[:, hs * 4:(hs + 1) * 4, :], ldS[b])
                    wld = pld
                    for mc in range(4):
                        pbk = pE[mc % 2]
                        for k in range(16):
                            w = list(cur_h) + [(ldS[b], wld), pE_ev[mc % 2]] \
                                if k == 0 else []
                            st.op('tensor', 'matmul', pbk[:, :],
                                  ws[b][:, k, mc * 128:(mc + 1) * 128],
                                  hbf[:, k, :], start=(k == 0), stop=(k == 15),
                                  wait=w, inc=(mm, 1) if k == 15 else None)
                        st.op('vector', 'tensor_scalar_add', ebf[:, mc, :],
                              pbk[:, :],
                              entb[:, l * 64 + hs * 4 + mc:l * 64 + hs * 4 + mc + 1],
                              wait=[(mm, mm.n), (ev_v, ev_v.n)], inc=(ev_v, 1))
                        pE_ev[mc % 2] = (ev_v, ev_v.n)
                    e_ready = ev_v.n
                    for dc in range(16):
                        db = dc % 4
                        for kk in range(4):
                            w = [(ev_v, e_ready), (ldS[b], pld), pP_ev[db]] \
                                if kk == 0 else []
                            st.op('tensor', 'matmul', pP[db][:, :],
                                  ps[b][:, kk, dc * 128:(dc + 1) * 128],
                                  ebf[:, kk, :], start=(kk == 0), stop=(kk == 3),
                                  wait=w, inc=(mm, 1) if kk == 3 else None)
                        if hs == 0:
                            st.op('vector', 'tensor_copy', y[:, dc, :], pP[db][:, :],
                                  wait=[(mm, mm.n)], inc=(ev_v, 1))
                        else:
                            st.op('vector', 'tensor_add', y[:, dc, :], y[:, dc, :],
                                  pP[db][:, :], wait=[(mm, mm.n)], inc=(ev_v, 1))
                        pP_ev[db] = (ev_v, ev_v.n)
                    wu[b] = (mm, mm.n)
                for ch in range(16):
                    st.op('vector', 'scalar_tensor_tensor', y[:, ch, :], y[:, ch, :],
                          entpb[:, l * 16 + ch:l * 16 + ch + 1], hf[:, ch, :],
                          ALU.add, ALU.add, wait=[(ev_v, ev_v.n)] + list(h_ready),
                          inc=(ev_v, 1))
                emit_ln(pb, st, L, y, hf, wait_x=[(ev_v, ev_v.n)],
                        wait_xhat=[(gh, hld)], gb=(ln3g, ln3b, l * 16))
                st.op('vector', 'tensor_copy', hbf[:, :, :], hf[:, :, :],
                      wait=[(ev_v, ev_v.n), (mm, mm.n)], inc=(ev_v, 1))
                hb_ready = ev_v.n
                cur_h = [(ev_v, hb_ready)]
                for sl in range(4):
                    b = sl % 2
                    wld = st.dma('sync', ws[b][:, :, :],
                                 fm3(d[wcn][l])[:, :, sl * 512:(sl + 1) * 512],
                                 ldS[b], wait=[wu[b]])
                    for mc in range(4):
                        ch = sl * 4 + mc
                        pbk = pE[mc % 2]
                        for k in range(16):
                            w = [(ev_v, hb_ready), (ldS[b], wld), pE_ev[mc % 2]] \
                                if k == 0 else []
                            st.op('tensor', 'matmul', pbk[:, :],
                                  ws[b][:, k, mc * 128:(mc + 1) * 128],
                                  hbf[:, k, :], start=(k == 0), stop=(k == 15),
                                  wait=w, inc=(mm, 1) if k == 15 else None)
                        st.op('vector', 'scalar_tensor_tensor', y[:, ch, :],
                              pbk[:, :], bcomb[:, l * 16 + ch:l * 16 + ch + 1],
                              hf[:, ch, :], ALU.add, ALU.add,
                              wait=[(mm, mm.n), (ev_v, ev_v.n)], inc=(ev_v, 1))
                        pE_ev[mc % 2] = (ev_v, ev_v.n)
                    wu[b] = (mm, mm.n)
                emit_ln(pb, st, L, y, hf, wait_x=[(ev_v, ev_v.n)],
                        gb=(ln4g, ln4b, l * 16))
                if l == 0:
                    st.op('vector', 'tensor_copy', hbf[:, :, :], hf[:, :, :],
                          wait=[(ev_v, ev_v.n), (mm, mm.n)], inc=(ev_v, 1))
                    cur_h = [(ev_v, ev_v.n)]
            for ch in range(16):
                b = ch % 2
                xl = st.dma('sync', x3c[b][:, :], fm3(d['xr0'][s])[:, ch, t0:t0 + GSZ],
                            ldR[b], wait=[(ev_v, ev_v.n)])
                st.op('vector', 'tensor_add', x4t[b][:, :], hf[:, ch, :],
                      x3c[b][:, :], wait=[(ldR[b], xl), (ev_v, ev_v.n), stx[b]],
                      inc=(ev_v, 1))
                st.dma('scalar', fm3(d['x_out'][s])[:, ch, t0:t0 + GSZ],
                       x4t[b][:, :], goQ[b], wait=[(ev_v, ev_v.n)])
                stx[b] = (goQ[b], goQ[b].n)
        for sm in (goQ[0], goQ[1]):
            st.wait('scalar', sm, sm.n)
        st.realize()


L1_WEIGHTS = [
    ('l1wq', [D, D], F32), ('l1wk', [D, D], F32), ('l1wv', [D, D], F32),
    ('l1wo', [D, D], F32), ('bv_row', [1, D], F32),
    ('ampw', [D, D], F32), ('phw', [D, D], F32),
    ('entw0', [D, HID], F32), ('entw1', [D, HID], F32),
    ('entpw0', [HID, D], F32), ('entpw1', [HID, D], F32),
    ('pathw', [2, 2, D, D], F32), ('phase_r', [4, D], F32),
]


def build_full(last='l1fb', out_kind='ExternalOutput', dbg_all=False):
    pb = PB()
    nc = pb.nc
    E = 'ExternalOutput' if dbg_all else 'Internal'
    pb.register_consts(ALL_CONSTS)
    pb.dt('x_in', [NSEQ, D, S], F32, kind='ExternalInput')
    for nm, shp, dt in CONST_INPUTS + L0_WEIGHTS + FFN_WEIGHTS + L1_WEIGHTS:
        pb.dt(nm, shp, dt, kind='ExternalInput')
    pb.dt('xhat_scr', [NSEQ, D, S], BF16, kind=E)
    pb.dt('q_nope_scr', [NSEQ, D, S], BF16)
    pb.dt('q_pe_scr', [NSEQ, 1024, S], BF16)
    pb.dt('xr0', [NSEQ, D, S], F32,
          kind='ExternalOutput' if (last == 'l1wo' or dbg_all) else 'Internal')
    pb.dt('xr1', [NSEQ, D, S], F32, kind=E)
    pb.dt('q1_scr', [NSEQ, D, S], BF16, kind=E)
    pb.dt('k_scr', [NSEQ, D, S], BF16, kind=E)
    pb.dt('v_scr', [NSEQ, S, D], BF16, kind=E)
    pb.dt('ao_scr', [NSEQ, D, S], BF16, kind=E)
    pb.dt('h_scr', [NSEQ, D, S], BF16, kind=E)
    pb.dt('wcomb_bf', [2, D, D], BF16, kind=E)
    if last == 'l1fb':
        pb.dt('x_out', [NSEQ, D, S], F32, kind=out_kind)
    casts = l0_casts(pb)
    pb.dt('wg_bf', [D, HID], BF16)
    pb.dt('wu_bf', [D, HID], BF16)
    pb.dt('wd_bf', [HID, D], BF16)
    casts += [('wg_t', 'wg_bf', 'ffn'), ('wu_t', 'wu_bf', 'ffn'),
              ('wd_t', 'wd_bf', 'ffn')]
    for nm in ('l1wq', 'l1wk', 'l1wv', 'l1wo'):
        pb.dt(nm + '_bf', [D, D], BF16)
        casts.append((nm, nm + '_bf', 'l1a'))
    for nm in ('ampw', 'phw'):
        pb.dt(nm + '_bf', [D, D], BF16)
        casts.append((nm, nm + '_bf', 'l1f'))
    for nm in ('entw0', 'entw1'):
        pb.dt(nm + '_bf', [D, HID], BF16)
        casts.append((nm, nm + '_bf', 'l1f'))
    for nm in ('entpw0', 'entpw1'):
        pb.dt(nm + '_bf', [HID, D], BF16)
        casts.append((nm, nm + '_bf', 'l1f'))
    stage_init(pb, casts=casts)
    for s in range(NSEQ):
        stage_l0a1(pb, s)
        with ExitStack() as ckv:
            kvb = {
                'kv_tm': ckv.enter_context(nc.sbuf_tensor(f'kv_tm_{s}', [128, 16, 512], BF16)),
                'kv_fm': ckv.enter_context(nc.sbuf_tensor(f'kv_fm_{s}', [128, 4, 2048], BF16)),
                'kpe_pre': ckv.enter_context(nc.sbuf_tensor(f'kpe_pre_{s}', [64, 2048], BF16)),
                'kpe_fm': ckv.enter_context(nc.sbuf_tensor(f'kpe_fm_{s}', [64, 2048], BF16)),
            }
            stage_l0a2(pb, s, kvb)
            stage_l0b(pb, s, kvb)
    stage_ffn0(pb)
    stage_l1p(pb, 'q')
    stage_l1p(pb, 'k')
    stage_l1p(pb, 'v')
    for s in range(NSEQ):
        for half in range(2):
            stage_l1b(pb, s, half)
    stage_l1wo(pb)
    if last == 'l1fb':
        stage_wcomb(pb)
        stage_l1fa(pb)
        stage_l1fb(pb)
    for sm in pb.sems.values():
        assert sm.n < 60000, (sm.name, sm.n)
    
    pb.nc.finalize()
    return pb



_CACHED = {}


def kernel(**inputs):
    sh, pc = host_prep(inputs)
    if 'pb' not in _CACHED:
        _CACHED['pb'] = build_full('l1fb')
    pb = _CACHED['pb']
    names = {k for k, v in pb.kinds.items() if v == 'ExternalInput'}
    in_maps = [{k: (pc[c][k] if k in pc[c] else sh[k]) for k in names}
               for c in range(8)]
    res = run_bass_kernel_spmd(pb.nc, in_maps, core_ids=list(range(8)))
    out = np.empty((16, S, D), np.float32)
    for c in range(8):
        xo = np.asarray(res.results[c]['x_out'])
        out[c * NSEQ:(c + 1) * NSEQ] = xo.transpose(0, 2, 1)
    return out
